# version 31
# baseline (speedup 1.0000x reference)
import numpy as np
import concourse.bass as bass
import concourse.mybir as mybir
from concourse.bass_utils import run_bass_kernel_spmd

F32 = mybir.dt.float32
BF16 = mybir.dt.bfloat16
AF = mybir.ActivationFunctionType
ALU = mybir.AluOpType

D = 1024
NTOK = 8192
NSC = 16
T = 512
NH = 16
DH = 64
FDIM = 256
NDC = 8
NTC = 4


class _Res:
    __slots__ = ("w", "readers")

    def __init__(self):
        self.w = None
        self.readers = []


class _Op:
    __slots__ = ("eng", "fn", "waits", "dma", "mark", "semval", "dmarank", "label")

    def __init__(self, eng, fn, waits, dma):
        self.eng = eng
        self.fn = fn
        self.waits = waits
        self.dma = dma
        self.mark = False
        self.semval = None
        self.dmarank = None


class _Plan:
    def __init__(self):
        self.ops = []
        self.label = ""

    def op(self, eng, fn, reads=(), writes=(), dma=False):
        i = len(self.ops)
        waits = set()
        for r in reads:
            if r.w is not None:
                wo = self.ops[r.w]
                if dma or wo.dma or wo.eng != eng:
                    waits.add(r.w)
                    wo.mark = True
            r.readers.append(i)
        for r in writes:
            deps = ([] if r.w is None else [r.w]) + r.readers
            for d in deps:
                if d == i:
                    continue
                do = self.ops[d]
                if dma or do.dma or do.eng != eng:
                    waits.add(d)
                    do.mark = True
            r.w = i
            r.readers = []
        o = _Op(eng, fn, sorted(waits), dma)
        o.label = self.label
        self.ops.append(o)
        return i

    def finalize(self):
        cnt = {"tensor": 0, "vector": 0, "scalar": 0}
        dcnt = {"sync": 0, "gpsimd": 0, "scalar_dma": 0}
        for o in self.ops:
            if o.dma:
                dcnt[o.eng if o.eng != "scalar" else "scalar_dma"] += 1
                o.dmarank = dcnt[o.eng if o.eng != "scalar" else "scalar_dma"]
                o.mark = True
            elif o.eng in cnt:
                if o.mark:
                    cnt[o.eng] += 1
                    o.semval = cnt[o.eng]


def build_nc():
    nc = bass.Bass("TRN2", target_bir_lowering=False)

    xT = nc.dram_tensor("xT", [D, NTOK], F32, kind="ExternalInput").ap()
    outT = nc.dram_tensor("outT", [D, NTOK], F32, kind="ExternalOutput").ap()
    d_wq = nc.dram_tensor("wq", [D, D], F32, kind="ExternalInput").ap()
    d_wk = nc.dram_tensor("wk", [D, D], F32, kind="ExternalInput").ap()
    d_wv = nc.dram_tensor("wv", [D, D], F32, kind="ExternalInput").ap()
    d_wproj = nc.dram_tensor("wproj", [D, D], F32, kind="ExternalInput").ap()
    d_wffnT = nc.dram_tensor("wffnT", [D, D], F32, kind="ExternalInput").ap()
    d_vr = nc.dram_tensor("vr", [D, 32], F32, kind="ExternalInput").ap()
    d_cst128 = nc.dram_tensor("cst128", [128, 161], F32, kind="ExternalInput").ap()
    d_cstb = nc.dram_tensor("cstb", [128, 513], F32, kind="ExternalInput").ap()
    d_cst32 = nc.dram_tensor("cst32", [32, 2048], F32, kind="ExternalInput").ap()
    d_cst1 = nc.dram_tensor("cst1", [1, 1152], F32, kind="ExternalInput").ap()
    d_rows = nc.dram_tensor("rows_scratch", [NSC, 4, T], BF16, kind="Internal").ap()
    d_xn = nc.dram_tensor("xn_scratch", [D, NTOK], BF16, kind="Internal").ap()

    plan = _Plan()
    R = _Res

    import contextlib
    _es = contextlib.ExitStack()
    with _es:
        W1 = _es.enter_context(nc.sbuf_tensor("W1", [128, NDC, D], BF16))
        W2 = _es.enter_context(nc.sbuf_tensor("W2", [128, NDC, D], BF16))
        W3 = _es.enter_context(nc.sbuf_tensor("W3", [128, NDC, D], BF16))
        vr_sb = _es.enter_context(nc.sbuf_tensor("vr_sb", [128, NDC, 32], BF16))
        cst128 = _es.enter_context(nc.sbuf_tensor("cst128_sb", [128, 161], F32))
        cstb = _es.enter_context(nc.sbuf_tensor("cstb_sb", [128, 513], BF16))
        cst32 = _es.enter_context(nc.sbuf_tensor("cst32_sb", [32, 2048], BF16))
        cst1 = _es.enter_context(nc.sbuf_tensor("cst1_sb", [1, 1152], BF16))
        bvb_sb = _es.enter_context(nc.sbuf_tensor("bvb_sb", [128, D], BF16))
        idn_sb = cst128[:, 0:128]
        bq_sb = cst128[:, 128:136]
        bk_sb = cst128[:, 136:144]
        bpj_sb = cst128[:, 144:152]
        bfn_sb = cst128[:, 152:160]
        wqf_sb = cstb[:, 0:FDIM]
        wkf_sb = cstb[:, FDIM:2 * FDIM]
        oc_sb = cstb[:, 512:513]
        e32_sb = cst32[:, 0:NH * DH]
        ct_sb = cst32[:, NH * DH:2 * NH * DH]
        bv_sb = cst1[:, 0:D]
        or_sb = cst1[:, D:D + 128]
        xb0 = _es.enter_context(nc.sbuf_tensor("xb0", [128, NDC, T], F32))
        xb1 = _es.enter_context(nc.sbuf_tensor("xb1", [128, NDC, T], F32))
        xbf0 = _es.enter_context(nc.sbuf_tensor("xbf0", [128, T], BF16))
        xbf1 = _es.enter_context(nc.sbuf_tensor("xbf1", [128, T], BF16))
        sqr0 = _es.enter_context(nc.sbuf_tensor("sqr0", [128, T], BF16))
        sqr1 = _es.enter_context(nc.sbuf_tensor("sqr1", [128, T], BF16))
        rows_f = _es.enter_context(nc.sbuf_tensor("rows_f", [1, 3, T], F32))
        rows_b = _es.enter_context(nc.sbuf_tensor("rows_b", [1, 4, T], BF16))
        tmpA = _es.enter_context(nc.sbuf_tensor("tmpA", [128, T], F32))
        bsum_sb = _es.enter_context(nc.sbuf_tensor("bsum_sb", [128, 8], F32))
        xn_sb = _es.enter_context(nc.sbuf_tensor("xn_sb", [128, NDC, T], BF16))
        kt_sb = _es.enter_context(nc.sbuf_tensor("kt_sb", [128, NDC, T], BF16))
        vaug = _es.enter_context(nc.sbuf_tensor("vaug", [128, NTC, NH, 65], BF16))
        kp0 = _es.enter_context(nc.sbuf_tensor("kp0", [128, T], BF16))
        kp1 = _es.enter_context(nc.sbuf_tensor("kp1", [128, T], BF16))
        kp2 = _es.enter_context(nc.sbuf_tensor("kp2", [128, T], BF16))
        kp3 = _es.enter_context(nc.sbuf_tensor("kp3", [128, T], BF16))
        qps_all = [_es.enter_context(nc.sbuf_tensor(f"qp{i}", [128, T], BF16))
                   for i in range(6)]
        ctxacc = _es.enter_context(nc.sbuf_tensor("ctxacc", [128, NH * FDIM], F32))
        ctxF0 = _es.enter_context(nc.sbuf_tensor("ctxF0", [128, NH * 65], BF16))
        ctxF1 = _es.enter_context(nc.sbuf_tensor("ctxF1", [128, NH * 65], BF16))
        n16 = _es.enter_context(nc.sbuf_tensor("n16", [65, NH, T], BF16))
        D16 = _es.enter_context(nc.sbuf_tensor("D16", [100, T], F32))
        Di16 = _es.enter_context(nc.sbuf_tensor("Di16", [100, T], F32))
        HI16 = _es.enter_context(nc.sbuf_tensor("HI16", [100, T], BF16))
        LO16 = _es.enter_context(nc.sbuf_tensor("LO16", [100, T], BF16))
        HILO = _es.enter_context(nc.sbuf_tensor("HILO", [32, T], BF16))
        yT = _es.enter_context(nc.sbuf_tensor("yT", [128, NDC, T], BF16))
        yodd = _es.enter_context(nc.sbuf_tensor("yodd", [64, NDC, T], BF16))
        sT_sb = _es.enter_context(nc.sbuf_tensor("sT_sb", [32, T], BF16))
        osb0 = _es.enter_context(nc.sbuf_tensor("osb0", [128, T], F32))
        osb1 = _es.enter_context(nc.sbuf_tensor("osb1", [128, T], F32))
        pa = _es.enter_context(nc.psum_tensor("pa", [128, T], F32))
        pb = _es.enter_context(nc.psum_tensor("pb", [128, T], F32))
        pc = _es.enter_context(nc.psum_tensor("pc", [128, T], F32))
        pd = _es.enter_context(nc.psum_tensor("pd", [128, T], F32))
        prb = _es.enter_context(nc.psum_tensor("prb", [128, T], F32))
        pmrb = _es.enter_context(nc.psum_tensor("pmrb", [128, T], F32))
        px1 = _es.enter_context(nc.psum_tensor("px1", [128, T], F32))
        px2 = _es.enter_context(nc.psum_tensor("px2", [128, T], F32))
        s_t = _es.enter_context(nc.semaphore("s_t"))
        s_v = _es.enter_context(nc.semaphore("s_v"))
        s_s = _es.enter_context(nc.semaphore("s_s"))
        s_x = _es.enter_context(nc.semaphore("s_x"))
        s_g = _es.enter_context(nc.semaphore("s_g"))
        blk = _es.enter_context(nc.Block())

        rW1 = R(); rW2 = R(); rW3 = R(); rVr = R()
        rC128 = R(); rCB = R(); rC32 = R(); rC1 = R()
        rBvb = R()
        rXb = [[R() for _ in range(NDC)] for _ in range(2)]
        rXbf = [R(), R()]; rSqr = [R(), R()]
        rRowF = [R() for _ in range(4)]
        rRowB = [R() for _ in range(4)]
        rTmpA = R(); rBsum = R()
        rXn = [R() for _ in range(NDC)]
        rXnD = [[R() for _ in range(NDC)] for _ in range(NSC)]
        rKt = [R() for _ in range(NDC)]
        rVa = [[R(), R()] for _ in range(NTC)]
        rKp = [R(), R(), R(), R()]
        rQp = [R() for _ in range(6)]
        rCtx = R(); rCtxF = [R(), R()]
        rN16 = [R() for _ in range(NH)]
        rD16 = R(); rDi = R(); rHI = R(); rLO = R(); rHILO = [R() for _ in range(4)]
        rYe = [R() for _ in range(NDC)]
        rYo = [R() for _ in range(NDC)]
        rYoD = [R(), R()]
        rST = R()
        rOsb = [R(), R()]
        rOut = R()
        rPa = R(); rPb = R(); rPc = R(); rPd = R()
        rPrb = R(); rPmrb = R(); rPx1 = R(); rPx2 = R()

        def dma(eng, dst, src, reads, writes):
            plan.op(eng, lambda e, d=dst, s=src: e.dma_start(d, s), reads, writes, dma=True)

        dma("sync", cst128[:], d_cst128, [], [rC128])
        for dc in range(NDC):
            dma("gpsimd", W1[:, dc, :], d_wk[128 * dc:128 * (dc + 1), :], [], [rW1])
            dma("gpsimd", W2[:, dc, :], d_wv[128 * dc:128 * (dc + 1), :], [], [rW2])
            dma("gpsimd", vr_sb[:, dc, :], d_vr[128 * dc:128 * (dc + 1), :], [], [rVr])
        dma("gpsimd", cstb[:], d_cstb, [], [rCB])
        dma("gpsimd", cst32[:], d_cst32, [], [rC32])
        dma("gpsimd", cst1[:], d_cst1, [], [rC1])

        plan.op("vector", lambda e: e.memset(ctxacc[:], 0.0), [], [rCtx])
        plan.op("vector", lambda e: e.tensor_add(bsum_sb[:], bpj_sb, bfn_sb),
                [rC128], [rBsum])
        plan.op("vector", lambda e: e.memset(HILO[:], 0.0), [], list(rHILO))
        plan.op("vector", lambda e: e.memset(vaug[:, :, :, 64:65], 1.0), [rC1],
                [r for pair in rVa for r in pair])
        for half, ph, pr in ((0, px1, rPx1), (1, px2, rPx2)):
            plan.op("tensor",
                    lambda e, ph=ph, half=half: e.matmul(
                        ph[:], or_sb, bv_sb[0:1, 512 * half:512 * (half + 1)],
                        start=True, stop=True),
                    [rC1], [pr])
            plan.op("vector",
                    lambda e, ph=ph, half=half: e.tensor_copy(
                        bvb_sb[:, 512 * half:512 * (half + 1)], ph[:]),
                    [pr], [rBvb])


        def g_stats(s, pass2=False):
            xb, rXbc = (xb0, rXb[0]) if s % 2 == 0 else (xb1, rXb[1])
            yield ("dma",)
            for c in range(NDC):
                dma("sync", xb[:, c, :], xT[128 * c:128 * (c + 1), T * s:T * (s + 1)],
                    [], [rXbc[c]])
            if pass2:
                yield ("spacer",)
                return
            for cg in range(NDC // 2):
                yield ("stats", cg)
                plan.label = f"stats{s}"
                for c in (2 * cg, 2 * cg + 1):
                    xr, sr = (xbf0, sqr0) if c % 2 == 0 else (xbf1, sqr1)
                    rxr, rsr = rXbf[c % 2], rSqr[c % 2]
                    plan.op("scalar", lambda e, xr=xr, xb=xb, c=c: e.copy(
                        xr[:], xb[:, c, :]), [rXbc[c]], [rxr])
                    plan.op("vector", lambda e, sr=sr, xb=xb, c=c: e.tensor_mul(
                        sr[:], xb[:, c, :], xb[:, c, :]), [rXbc[c]], [rsr])
                    plan.op("tensor", lambda e, xr=xr, c=c: e.matmul(
                        prb[0:1, :], oc_sb, xr[:], start=(c == 0), stop=(c == NDC - 1)),
                        [rxr, rCB], [rPrb])
                    plan.op("tensor", lambda e, sr=sr, c=c: e.matmul(
                        pmrb[0:1, :], oc_sb, sr[:], start=(c == 0), stop=(c == NDC - 1)),
                        [rsr, rCB], [rPmrb])
            yield ("rows",)
            plan.label = f"rows{s}"
            mean = rows_f[0:1, 0, :]; slot1 = rows_f[0:1, 1, :]
            slot2 = rows_f[0:1, 2, :]
            plan.op("vector", lambda e: e.tensor_scalar_mul(mean, prb[0:1, :], 1.0 / D),
                    [rPrb], [rRowF[0]])
            plan.op("vector", lambda e: e.tensor_scalar(
                slot1, pmrb[0:1, :], 1.0 / D, 1e-5, ALU.mult, ALU.add),
                [rPmrb], [rRowF[1]])
            plan.op("vector", lambda e: e.tensor_mul(slot2, mean, mean),
                    [rRowF[0]], [rRowF[2]])
            plan.op("vector", lambda e: e.tensor_sub(slot1, slot1, slot2),
                    [rRowF[1], rRowF[2]], [rRowF[1]])
            plan.op("scalar", lambda e: e.activation(slot2, slot1, AF.Ln),
                    [rRowF[1]], [rRowF[2]])
            plan.op("scalar", lambda e: e.activation(slot1, slot2, AF.Exp, scale=-0.5),
                    [rRowF[2]], [rRowF[1]])
            plan.op("vector", lambda e: e.tensor_mul(slot2, mean, slot1),
                    [rRowF[0], rRowF[1]], [rRowF[2]])
            rstd = slot1; mr = slot2
            rhi = rows_b[0:1, 0, :]; rlo = rows_b[0:1, 1, :]
            mhi = rows_b[0:1, 2, :]; mlo = rows_b[0:1, 3, :]
            plan.op("vector", lambda e: e.tensor_copy(rhi, rstd), [rRowF[1]], [rRowB[0]])
            plan.op("vector", lambda e: e.tensor_sub(rlo, rstd, rhi),
                    [rRowF[1], rRowB[0]], [rRowB[1]])
            plan.op("vector", lambda e: e.tensor_copy(mhi, mr), [rRowF[2]], [rRowB[2]])
            plan.op("vector", lambda e: e.tensor_sub(mlo, mr, mhi),
                    [rRowF[2], rRowB[2]], [rRowB[3]])
            yield ("spacer",)
            yield ("bcast",)
            plan.label = f"bcast{s}"
            plan.op("tensor", lambda e: e.matmul(prb[:], or_sb, rhi, start=True, stop=False),
                    [rRowB[0], rC1], [rPrb])
            plan.op("tensor", lambda e: e.matmul(prb[:], or_sb, rlo, start=False, stop=True),
                    [rRowB[1], rC1], [rPrb])
            plan.op("tensor", lambda e: e.matmul(pmrb[:], or_sb, mhi, start=True, stop=False),
                    [rRowB[2], rC1], [rPmrb])
            plan.op("tensor", lambda e: e.matmul(pmrb[:], or_sb, mlo, start=False, stop=True),
                    [rRowB[3], rC1], [rPmrb])

        def g_apply(s):
            xb, rXbc = (xb0, rXb[0]) if s % 2 == 0 else (xb1, rXb[1])
            yield ("apply", 0)
            plan.label = f"apply{s}"
            for c in range(NDC):
                plan.op("vector", lambda e, xb=xb, c=c: e.tensor_mul(
                    tmpA[:], xb[:, c, :], prb[:]), [rXbc[c], rPrb], [rTmpA])
                plan.op("vector", lambda e, c=c: e.tensor_sub(
                    xn_sb[:, c, :], tmpA[:], pmrb[:]),
                    [rTmpA, rPmrb], [rXn[c]])
                dma("sync", d_xn[128 * c:128 * (c + 1), T * s:T * (s + 1)],
                    xn_sb[:, c, :], [rXn[c]], [rXnD[s][c]])

        def g_proj(proj_w, rProjW, proj_bias):
            for oc in range(NDC):
                ph, pr = (pa, rPa) if oc % 2 == 0 else (pb, rPb)
                for dc in range(NDC):
                    if dc % 2 == 0:
                        yield ("kT", oc)
                        plan.label = f"kT{oc}"
                    plan.op("tensor", lambda e, ph=ph, dc=dc, oc=oc: e.matmul(
                        ph[:], proj_w[:, dc, 128 * oc:128 * (oc + 1)], xn_sb[:, dc, :],
                        start=(dc == 0), stop=(dc == NDC - 1)),
                        [rProjW, rXn[dc]], [pr])
                plan.op("vector", lambda e, ph=ph, oc=oc: e.tensor_scalar(
                    kt_sb[:, oc, :], ph[:], proj_bias[:, oc:oc + 1], None, ALU.add),
                    [pr, rC128], [rKt[oc]])

        def g_v():
            for half in range(2):
                for tc in range(NTC):
                    ph, pr = (pa, rPa) if (tc + half) % 2 == 0 else (pb, rPb)
                    for dc in range(NDC):
                        if dc % 2 == 0:
                            yield ("v", half, tc)
                            plan.label = f"v{half}{tc}"
                        plan.op("tensor", lambda e, ph=ph, tc=tc, half=half, dc=dc: e.matmul(
                            ph[:], xn_sb[:, dc, 128 * tc:128 * (tc + 1)],
                            W2[:, dc, 512 * half:512 * (half + 1)],
                            start=(dc == 0), stop=(dc == NDC - 1)),
                            [rW2, rXn[dc]], [pr])
                    plan.op("vector", lambda e, ph=ph, tc=tc, half=half: e.tensor_add(
                        vaug[:, tc, 8 * half:8 * (half + 1), 0:64],
                        ph[:].rearrange("p (h x) -> p h x", h=8),
                        bvb_sb[:, 512 * half:512 * (half + 1)].rearrange(
                            "p (h x) -> p h x", h=8)),
                        [pr, rBvb], [rVa[tc][half]])

        def g_qt2(s):
            xb, rXbc = (xb0, rXb[0]) if s % 2 == 0 else (xb1, rXb[1])
            qbanks = ((pa, rPa), (pb, rPb), (px1, rPx1), (px2, rPx2))
            for ocb in range(2):
                for dc in range(NDC):
                    yield ("qt", ocb, dc)
                    for j in range(4):
                        oc = 4 * ocb + j
                        ph, pr = qbanks[j]
                        plan.op("tensor", lambda e, ph=ph, dc=dc, oc=oc: e.matmul(
                            ph[:], W1[:, dc, 128 * oc:128 * (oc + 1)], xn_sb[:, dc, :],
                            start=(dc == 0), stop=(dc == NDC - 1)),
                            [rW1, rXn[dc]], [pr])
                yield ("qtev", ocb)
                for j in range(4):
                    oc = 4 * ocb + j
                    ph, pr = qbanks[j]
                    plan.op("vector", lambda e, ph=ph, oc=oc: e.tensor_scalar(
                        kt_sb[:, oc, :], ph[:], bq_sb[:, oc:oc + 1], None, ALU.add),
                        [pr, rC128], [rKt[oc]])
            yield ("sT",)
            for dc in range(NDC):
                plan.op("tensor", lambda e, dc=dc: e.matmul(
                    pa[0:32, :], vr_sb[:, dc, :], xn_sb[:, dc, :],
                    start=(dc == 0), stop=(dc == NDC - 1)),
                    [rVr, rXn[dc]], [rPa])
            plan.op("vector", lambda e: e.tensor_copy(sT_sb[:], pa[0:32, :]),
                    [rPa], [rST])

        def g_sT():
            yield ("sT",)
            for dc in range(NDC):
                plan.op("tensor", lambda e, dc=dc: e.matmul(
                    pa[0:32, :], vr_sb[:, dc, :], xn_sb[:, dc, :],
                    start=(dc == 0), stop=(dc == NDC - 1)),
                    [rVr, rXn[dc]], [rPa])
            plan.op("vector", lambda e: e.tensor_copy(sT_sb[:], pa[0:32, :]),
                    [rPa], [rST])

        def _feat_ctx(h, tcg, kpx, rkpx):
            ctxp, rctxp = (px1, rPx1) if h % 2 == 0 else (px2, rPx2)
            for j in range(2):
                tc = 2 * tcg + j
                plan.op("tensor", lambda e, ctxp=ctxp, tc=tc, h=h, kpx=kpx, j=j: e.matmul(
                    ctxp[0:65, 0:FDIM], vaug[:, tc, h, :],
                    kpx[:, FDIM * j:FDIM * (j + 1)],
                    start=(tc == 0), stop=(tc == NTC - 1)),
                    [rVa[tc][h // 8], rkpx], [rctxp])
            if tcg == 1:
                plan.op("vector", lambda e, ctxp=ctxp, h=h: e.tensor_add(
                    ctxacc[0:65, FDIM * h:FDIM * (h + 1)],
                    ctxacc[0:65, FDIM * h:FDIM * (h + 1)], ctxp[0:65, 0:FDIM]),
                    [rctxp, rCtx], [rCtx])

        def g_feats():
            from collections import deque
            pend = deque()
            it = 0
            for h in range(NH):
                pr_i, hh = h // 2, h % 2
                for tcg in range(2):
                    yield ("feat", it)
                    plan.label = f"feat{it}"
                    kfp, rkfp = (pc, rPc) if it % 2 == 0 else (pd, rPd)
                    kpx, rkpx = ((kp0, rKp[0]), (kp1, rKp[1]), (kp2, rKp[2]), (kp3, rKp[3]))[it % 4]
                    for j in range(2):
                        tc = 2 * tcg + j
                        plan.op("tensor", lambda e, kfp=kfp, hh=hh, pr_i=pr_i, tc=tc, j=j: e.matmul(
                            kfp[:, FDIM * j:FDIM * (j + 1)],
                            kt_sb[64 * hh:64 * (hh + 1), pr_i, 128 * tc:128 * (tc + 1)],
                            wkf_sb[64 * hh:64 * (hh + 1), :],
                            start=True, stop=True),
                            [rKt[pr_i], rCB], [rkfp])
                    plan.op("scalar", lambda e, kpx=kpx, kfp=kfp: e.activation(
                        kpx[:], kfp[:], AF.Exp), [rkfp], [rkpx])
                    pend.append((h, tcg, kpx, rkpx))
                    if len(pend) > 2:
                        _feat_ctx(*pend.popleft())
                    it += 1
            while pend:
                yield ("featdrain", it)
                plan.label = f"fdrain{it}"
                it += 1
                _feat_ctx(*pend.popleft())

        def g_trans():
            yield ("wload",)
            for dc in range(NDC):
                dma("gpsimd", W1[:, dc, :], d_wq[128 * dc:128 * (dc + 1), :], [rW1], [rW1])
                dma("gpsimd", W2[:, dc, :], d_wproj[128 * dc:128 * (dc + 1), :], [rW2], [rW2])
                dma("gpsimd", W3[:, dc, :], d_wffnT[128 * dc:128 * (dc + 1), :], [rW3], [rW3])
            for h in range(NH):
                yield ("tp", h)
                for fc in range(2):
                    ph, pr = (pa, rPa) if fc == 0 else (pb, rPb)
                    ctxF, rcF = (ctxF0, rCtxF[0]) if fc == 0 else (ctxF1, rCtxF[1])
                    plan.op("tensor", lambda e, ph=ph, h=h, fc=fc: e.transpose(
                        ph[:, 0:128], ctxacc[:, FDIM * h + 128 * fc:FDIM * h + 128 * (fc + 1)],
                        idn_sb), [rCtx, rC128], [pr])
                    plan.op("vector", lambda e, ph=ph, ctxF=ctxF, h=h: e.tensor_copy(
                        ctxF[:, 65 * h:65 * (h + 1)], ph[:, 0:65]), [pr], [rcF])

        def ffn_part_t0():
            for dc in range(NDC):
                plan.op("tensor", lambda e, dc=dc: e.matmul(
                    px1[:], W3[:, dc, 256:384], xn_sb[:, dc, :],
                    start=(dc == 0), stop=False),
                    [rW3, rXn[dc]], [rPx1])
            plan.op("tensor", lambda e: e.matmul(
                px1[:], ct_sb[:, 256:384], sT_sb[:],
                start=False, stop=False),
                [rC32, rST], [rPx1])

        def heads_front(h):
            pr_i, hh = h // 2, h % 2
            banks = ((pc, rPc), (pd, rPd)) if h % 2 == 0 else ((px1, rPx1), (px2, rPx2))
            for fc in range(2):
                qfp, rqfp = banks[fc]
                qpx = qps_all[2 * (h % 3) + fc]
                rqpx = rQp[2 * (h % 3) + fc]
                plan.op("tensor", lambda e, qfp=qfp, hh=hh, fc=fc, pr_i=pr_i: e.matmul(
                    qfp[:], wqf_sb[64 * hh:64 * (hh + 1), 128 * fc:128 * (fc + 1)],
                    kt_sb[64 * hh:64 * (hh + 1), pr_i, :],
                    start=True, stop=True),
                    [rKt[pr_i], rCB], [rqfp])
                plan.op("scalar", lambda e, qpx=qpx, qfp=qfp: e.activation(
                    qpx[:], qfp[:], AF.Exp), [rqfp], [rqpx])

        def heads_numevac(h):
            nump, rnump = (prb, rPrb) if h % 2 == 0 else (pmrb, rPmrb)
            for fc in range(2):
                qpx = qps_all[2 * (h % 3) + fc]
                rqpx = rQp[2 * (h % 3) + fc]
                ctxF, rcF = (ctxF0, rCtxF[0]) if fc == 0 else (ctxF1, rCtxF[1])
                plan.op("tensor", lambda e, nump=nump, ctxF=ctxF, h=h, qpx=qpx, fc=fc: e.matmul(
                    nump[0:65, :], ctxF[:, 65 * h:65 * (h + 1)], qpx[:],
                    start=(fc == 0), stop=(fc == 1)),
                    [rcF, rqpx], [rnump])
            plan.op("vector", lambda e, nump=nump, h=h: e.tensor_copy(
                n16[:, h, :], nump[0:65, :]), [rnump], [rN16[h]])
            if h % 4 == 3:
                q4 = h // 4
                p0 = 32 * q4
                dma("gpsimd", D16[p0:p0 + 4, :],
                    n16[64:65, 4 * q4:4 * (q4 + 1), :],
                    [rN16[4 * q4], rN16[4 * q4 + 1], rN16[4 * q4 + 2], rN16[4 * q4 + 3]],
                    [rD16])

        def heads_rows(q4):
            sl = slice(32 * q4, 32 * q4 + 4)
            plan.op("vector", lambda e, sl=sl: e.tensor_scalar_add(
                Di16[sl, :], D16[sl, :], 1e-6), [rD16], [rDi])
            plan.op("scalar", lambda e, sl=sl: e.activation(
                D16[sl, :], Di16[sl, :], AF.Ln), [rDi], [rD16])
            plan.op("scalar", lambda e, sl=sl: e.activation(
                Di16[sl, :], D16[sl, :], AF.Exp, scale=-1.0), [rD16], [rDi])
            plan.op("vector", lambda e, sl=sl: e.tensor_copy(HI16[sl, :], Di16[sl, :]),
                    [rDi], [rHI])
            plan.op("vector", lambda e, sl=sl: e.tensor_sub(
                LO16[sl, :], Di16[sl, :], HI16[sl, :]), [rDi, rHI], [rLO])
            a = 4 * q4
            dma("gpsimd", HILO[a:a + 4, :], HI16[sl, :], [rHI], [rHILO[q4]])
            dma("gpsimd", HILO[16 + a:16 + a + 4, :], LO16[sl, :], [rLO], [rHILO[q4]])

        def warm_mm(n, bank, rbank):
            for _ in range(n):
                plan.op("tensor", lambda e, bank=bank: e.matmul(
                    bank[0:1, :], oc_sb, W1[:, 0, 0:T], start=True, stop=True),
                    [rW1, rCB], [rbank])

        def heads_sel_mult(h):
            dbp, rdbp = (prb, rPrb) if h % 2 == 0 else (pmrb, rPmrb)
            plan.op("tensor", lambda e, dbp=dbp, h=h: e.matmul(
                dbp[0:64, :], e32_sb[:, 64 * h:64 * (h + 1)], HILO[:],
                start=True, stop=True), [rC32, rHILO[h // 4]], [rdbp])
            oc = h // 2
            if h % 2 == 0:
                plan.op("vector", lambda e, dbp=dbp, h=h, oc=oc: e.tensor_mul(
                    yT[0:64, oc, :], n16[0:64, h, :], dbp[0:64, :]),
                    [rN16[h], rdbp], [rYe[oc]])
            else:
                plan.op("vector", lambda e, dbp=dbp, h=h, oc=oc: e.tensor_mul(
                    yodd[:, oc, :], n16[0:64, h, :], dbp[0:64, :]),
                    [rN16[h], rdbp], [rYo[oc]])

        def g_heads():
            for h in range(NH):
                yield ("head", h)
                heads_front(h)
                if 4 <= h < 12:
                    oc = (h - 4) // 4
                    ph, pr = (pa, rPa) if oc == 0 else (pb, rPb)
                    j0 = ((h - 4) % 4) * 2
                    for dc in (j0, j0 + 1):
                        if dc < NDC:
                            plan.op("tensor", lambda e, ph=ph, dc=dc, oc=oc: e.matmul(
                                ph[:], W3[:, dc, 128 * oc:128 * (oc + 1)], xn_sb[:, dc, :],
                                start=(dc == 0), stop=False),
                                [rW3, rXn[dc]], [pr])
                    if (h - 4) % 4 == 3:
                        plan.op("tensor", lambda e, ph=ph, oc=oc: e.matmul(
                            ph[:], ct_sb[:, 128 * oc:128 * (oc + 1)], sT_sb[:],
                            start=False, stop=False),
                            [rC32, rST], [pr])
                if h >= 2:
                    heads_numevac(h - 2)
                if h == NH - 1:
                    heads_numevac(h - 1)
                if h >= 5 and h % 4 == 1:
                    heads_rows((h - 5) // 4)
            yield ("t0",)
            heads_numevac(15)
            heads_rows(3)

        def g_out(s, st=None, s_next=None):
            xb, rXbc = (xb0, rXb[0]) if s % 2 == 0 else (xb1, rXb[1])
            obank = (pa, pb, px1, px2, pc, pd)
            orbank = (rPa, rPb, rPx1, rPx2, rPc, rPd)

            def ffn_part(oc, ph, pr):
                for dc in range(NDC):
                    plan.op("tensor", lambda e, ph=ph, dc=dc, oc=oc: e.matmul(
                        ph[:], W3[:, dc, 128 * oc:128 * (oc + 1)], xn_sb[:, dc, :],
                        start=(dc == 0), stop=False),
                        [rW3, rXn[dc]], [pr])
                plan.op("tensor", lambda e, ph=ph, oc=oc: e.matmul(
                    ph[:], ct_sb[:, 128 * oc:128 * (oc + 1)], sT_sb[:],
                    start=False, stop=False),
                    [rC32, rST], [pr])

            def proj_part(oc, ph, pr):
                for hc in range(NDC):
                    plan.op("tensor", lambda e, ph=ph, hc=hc, oc=oc: e.matmul(
                        ph[:], W2[:, hc, 128 * oc:128 * (oc + 1)], yT[:, hc, :],
                        start=False, stop=(hc == NDC - 1)),
                        [rW2, rYe[hc], rYoD[hc // 4]], [pr])

            def evac(oc, ph, pr):
                osb, rosb = (osb0, rOsb[0]) if oc % 2 == 0 else (osb1, rOsb[1])
                plan.op("vector", lambda e, ph=ph, osb=osb, xb=xb, oc=oc: e.scalar_tensor_tensor(
                    osb[:], ph[:], bsum_sb[:, oc:oc + 1], xb[:, oc, :],
                    ALU.add, ALU.add), [pr, rBsum, rXbc[oc]], [rosb])
                dma("sync", outT[128 * oc:128 * (oc + 1), T * s:T * (s + 1)], osb[:],
                    [rosb], [rOut])

            for oc in range(2, 6):
                yield ("outA", oc)
                if oc >= 3 and st is not None and st.alive:
                    st.step()
                ffn_part(oc, obank[oc], orbank[oc])
                if oc == 3:
                    for h in range(NH):
                        heads_sel_mult(h)
                        if h == 7:
                            dma("gpsimd", yT[64:128, 0:4, :], yodd[:, 0:4, :],
                                [rYo[0], rYo[1], rYo[2], rYo[3]], [rYoD[0]])
                    dma("gpsimd", yT[64:128, 4:8, :], yodd[:, 4:8, :],
                        [rYo[4], rYo[5], rYo[6], rYo[7]], [rYoD[1]])
            for oc in range(6):
                yield ("outB", oc)
                if st is not None and st.alive:
                    st.step()
                proj_part(oc, obank[oc], orbank[oc])
                evac(oc, obank[oc], orbank[oc])
                if oc == 2:
                    yield ("outCf", 6)
                    if st is not None and st.alive:
                        st.step()
                    ffn_part(6, pa, rPa)
                if oc == 3:
                    yield ("outCf", 7)
                    if st is not None and st.alive:
                        st.step()
                    ffn_part(7, pb, rPb)
                    if s_next is not None:
                        for c in range(NDC):
                            dma("gpsimd", xn_sb[:, c, :],
                                d_xn[128 * c:128 * (c + 1), T * s_next:T * (s_next + 1)],
                                [rXnD[s_next][c]], [rXn[c]])
            for oc in (6, 7):
                yield ("outC", oc)
                if st is not None and st.alive:
                    st.step()
                ph, pr = (pa, rPa) if oc == 6 else (pb, rPb)
                proj_part(oc, ph, pr)
                evac(oc, ph, pr)

        class Stepper:
            def __init__(self, gen):
                self.gen = gen
                self.alive = True
                self.tag = None
                self._adv()

            def _adv(self):
                try:
                    self.tag = next(self.gen)
                except StopIteration:
                    self.alive = False
                    self.tag = None

            def step(self):
                self._adv()

        def chain(*gens):
            for g in gens:
                for t in g:
                    yield t

        def drain(gen):
            for _ in gen:
                pass

        def drive(lgen, fgen, lready, wl, wf):
            l = Stepper(lgen); f = Stepper(fgen)
            cl = cf = 0.0
            fprog = 0
            while l.alive or f.alive:
                want_l = (l.alive and (not f.alive or cl * wf <= cf * wl)
                          and lready(l.tag, fprog, f.alive))
                if want_l:
                    l.step(); cl += 1
                elif f.alive:
                    f.step(); fprog += 1; cf += 1
                elif l.alive:
                    l.step(); cl += 1

        def p1_ready(tag, fprog, falive):
            if not falive:
                return True
            if tag[0] == "kT":
                return fprog >= 4 * tag[1] + 4
            if tag[0] == "v":
                return fprog >= (18, 34)[tag[1]]
            if tag[0] == "bcast":
                return fprog >= 8
            return True

        def tr_ready(tag, fprog, falive):
            if not falive:
                return True
            if tag[0] == "tp":
                return fprog >= min(34, 2 * tag[1] + 4)
            return True

        drain(chain(g_stats(0), g_apply(0), g_proj(W1, rW1, bk_sb), g_v()))
        for s in range(1, NSC):
            drive(chain(g_stats(s), g_apply(s), g_proj(W1, rW1, bk_sb), g_v()),
                  g_feats(), p1_ready, 71, 34)
        drive(g_trans(), g_feats(), tr_ready, 17, 32)
        drain(g_stats(0, pass2=True))
        for c in range(NDC):
            dma("gpsimd", xn_sb[:, c, :], d_xn[128 * c:128 * (c + 1), 0:T],
                [rXnD[0][c]], [rXn[c]])
        drain(g_qt2(0))
        for s in range(NSC):
            drain(g_heads())
            st = Stepper(g_stats(s + 1, pass2=True)) if s < NSC - 1 else None
            drain(g_out(s, st, s + 1 if s < NSC - 1 else None))
            if s < NSC - 1:
                while st.alive:
                    st.step()
                drain(g_qt2(s + 1))

        plan.op("sync", None, [rOut], [])

        plan.finalize()

        sem_of = {"tensor": s_t, "vector": s_v, "scalar": s_s,
                  "sync": s_x, "gpsimd": s_g}

        def resolve(i):
            o = plan.ops[i]
            if o.dma:
                return (o.eng, 16 * o.dmarank)
            return (o.eng, o.semval)

        def emit(engine_name, eng):
            hw = {}
            for o in plan.ops:
                if o.eng != engine_name:
                    continue
                targets = {}
                for wi in o.waits:
                    sname, val = resolve(wi)
                    if val is None:
                        continue
                    if sname == engine_name and not plan.ops[wi].dma:
                        continue
                    if val > targets.get(sname, -1):
                        targets[sname] = val
                for sname, val in sorted(targets.items()):
                    if val > hw.get(sname, 0):
                        eng.wait_ge(sem_of[sname], val)
                        hw[sname] = val
                if o.fn is None:
                    continue
                inst = o.fn(eng)
                if o.dma:
                    inst.then_inc(sem_of[o.eng], 16)
                elif o.mark:
                    inst.then_inc(sem_of[o.eng], 1)

        @blk.sync
        def _(sync):
            emit("sync", sync)

        @blk.gpsimd
        def _(gp):
            emit("gpsimd", gp)

        @blk.tensor
        def _(tn):
            emit("tensor", tn)

        @blk.vector
        def _(vc):
            emit("vector", vc)

        @blk.scalar
        def _(sc):
            emit("scalar", sc)

    return nc


_NC_CACHE = None


def kernel(L0_attn_x, L0_attn_Wq, L0_attn_bq, L0_attn_Wk, L0_attn_bk,
           L0_attn_Wv, L0_attn_bv, L0_attn_Wproj, L0_attn_bproj,
           L0_attn_wq_feat, L0_attn_wk_feat, L0_ffn_W, L0_ffn_b,
           L0_ffn_V_r, L0_ffn_C):
    global _NC_CACHE
    x = np.asarray(L0_attn_x, dtype=np.float32)
    f32 = lambda a: np.ascontiguousarray(np.asarray(a, dtype=np.float32))

    wq = f32(L0_attn_Wq); wk = f32(L0_attn_Wk); wv = f32(L0_attn_Wv)
    wproj = f32(L0_attn_Wproj)
    wffnT = np.ascontiguousarray(f32(L0_ffn_W).T)
    vr = f32(L0_ffn_V_r)
    ct = np.ascontiguousarray(f32(L0_ffn_C).T)
    wqf2 = np.ascontiguousarray(np.vstack([f32(L0_attn_wq_feat)] * 2))
    wkf2 = np.ascontiguousarray(np.vstack([f32(L0_attn_wk_feat)] * 2))

    def b2(v):
        return np.ascontiguousarray(f32(v).reshape(8, 128).T)

    bq2 = b2(L0_attn_bq); bk2 = b2(L0_attn_bk)
    bpj2 = b2(L0_attn_bproj); bfn2 = b2(L0_ffn_b)
    bv1 = f32(L0_attn_bv).reshape(1, D)

    e32 = np.zeros((32, NH * DH), np.float32)
    for h in range(NH):
        e32[h, DH * h:DH * (h + 1)] = 1.0
        e32[16 + h, DH * h:DH * (h + 1)] = 1.0
    cst128 = np.zeros((128, 161), np.float32)
    cst128[:, 0:128] = np.eye(128, dtype=np.float32)
    cst128[:, 128:136] = bq2
    cst128[:, 136:144] = bk2
    cst128[:, 144:152] = bpj2
    cst128[:, 152:160] = bfn2
    cst128[:, 160] = 1.0
    cstb = np.zeros((128, 513), np.float32)
    cstb[:, 0:256] = wqf2
    cstb[:, 256:512] = wkf2
    cstb[:, 512] = 1.0
    cst32 = np.zeros((32, 2048), np.float32)
    cst32[:, 0:1024] = e32
    cst32[:, 1024:2048] = ct
    cst1 = np.zeros((1, 1152), np.float32)
    cst1[0, 0:1024] = bv1[0]
    cst1[0, 1024:1152] = 1.0

    if _NC_CACHE is None:
        _NC_CACHE = build_nc()
    nc = _NC_CACHE

    shared = {
        "wq": wq, "wk": wk, "wv": wv, "wproj": wproj, "wffnT": wffnT,
        "vr": vr, "cst128": cst128, "cstb": cstb, "cst32": cst32, "cst1": cst1,
    }
    in_maps = []
    for b in range(8):
        m = dict(shared)
        m["xT"] = np.ascontiguousarray(x[b].T)
        in_maps.append(m)

    res = run_bass_kernel_spmd(nc, in_maps, core_ids=list(range(8)))
    out = np.empty((8, NTOK, D), np.float32)
    for b in range(8):
        out[b] = res.results[b]["outT"].T
    return out


# version 32
# speedup vs baseline: 1.0072x; 1.0072x over previous
import numpy as np
import concourse.bass as bass
import concourse.mybir as mybir
from concourse.bass_utils import run_bass_kernel_spmd

F32 = mybir.dt.float32
BF16 = mybir.dt.bfloat16
AF = mybir.ActivationFunctionType
ALU = mybir.AluOpType

D = 1024
NTOK = 8192
NSC = 16
T = 512
NH = 16
DH = 64
FDIM = 256
NDC = 8
NTC = 4


class _Res:
    __slots__ = ("w", "readers")

    def __init__(self):
        self.w = None
        self.readers = []


class _Op:
    __slots__ = ("eng", "fn", "waits", "dma", "mark", "semval", "dmarank", "label")

    def __init__(self, eng, fn, waits, dma):
        self.eng = eng
        self.fn = fn
        self.waits = waits
        self.dma = dma
        self.mark = False
        self.semval = None
        self.dmarank = None


class _Plan:
    def __init__(self):
        self.ops = []
        self.label = ""

    def op(self, eng, fn, reads=(), writes=(), dma=False):
        i = len(self.ops)
        waits = set()
        for r in reads:
            if r.w is not None:
                wo = self.ops[r.w]
                if dma or wo.dma or wo.eng != eng:
                    waits.add(r.w)
                    wo.mark = True
            r.readers.append(i)
        for r in writes:
            deps = ([] if r.w is None else [r.w]) + r.readers
            for d in deps:
                if d == i:
                    continue
                do = self.ops[d]
                if dma or do.dma or do.eng != eng:
                    waits.add(d)
                    do.mark = True
            r.w = i
            r.readers = []
        o = _Op(eng, fn, sorted(waits), dma)
        o.label = self.label
        self.ops.append(o)
        return i

    def finalize(self):
        cnt = {"tensor": 0, "vector": 0, "scalar": 0}
        dcnt = {"sync": 0, "gpsimd": 0, "scalar_dma": 0}
        for o in self.ops:
            if o.dma:
                dcnt[o.eng if o.eng != "scalar" else "scalar_dma"] += 1
                o.dmarank = dcnt[o.eng if o.eng != "scalar" else "scalar_dma"]
                o.mark = True
            elif o.eng in cnt:
                if o.mark:
                    cnt[o.eng] += 1
                    o.semval = cnt[o.eng]


def build_nc():
    nc = bass.Bass("TRN2", target_bir_lowering=False)

    xT = nc.dram_tensor("xT", [D, NTOK], F32, kind="ExternalInput").ap()
    outT = nc.dram_tensor("outT", [D, NTOK], F32, kind="ExternalOutput").ap()
    d_wq = nc.dram_tensor("wq", [D, D], F32, kind="ExternalInput").ap()
    d_wk = nc.dram_tensor("wk", [D, D], F32, kind="ExternalInput").ap()
    d_wv = nc.dram_tensor("wv", [D, D], F32, kind="ExternalInput").ap()
    d_wproj = nc.dram_tensor("wproj", [D, D], F32, kind="ExternalInput").ap()
    d_wffnT = nc.dram_tensor("wffnT", [D, D], F32, kind="ExternalInput").ap()
    d_vr = nc.dram_tensor("vr", [D, 32], F32, kind="ExternalInput").ap()
    d_cst128 = nc.dram_tensor("cst128", [128, 161], F32, kind="ExternalInput").ap()
    d_cstb = nc.dram_tensor("cstb", [128, 513], F32, kind="ExternalInput").ap()
    d_cst32 = nc.dram_tensor("cst32", [32, 2048], F32, kind="ExternalInput").ap()
    d_cst1 = nc.dram_tensor("cst1", [1, 1152], F32, kind="ExternalInput").ap()
    d_rows = nc.dram_tensor("rows_scratch", [NSC, 4, T], BF16, kind="Internal").ap()
    d_xn = nc.dram_tensor("xn_scratch", [D, NTOK], BF16, kind="Internal").ap()

    plan = _Plan()
    R = _Res

    import contextlib
    _es = contextlib.ExitStack()
    with _es:
        W1 = _es.enter_context(nc.sbuf_tensor("W1", [128, NDC, D], BF16))
        W2 = _es.enter_context(nc.sbuf_tensor("W2", [128, NDC, D], BF16))
        W3 = _es.enter_context(nc.sbuf_tensor("W3", [128, NDC, D], BF16))
        vr_sb = _es.enter_context(nc.sbuf_tensor("vr_sb", [128, NDC, 32], BF16))
        cst128 = _es.enter_context(nc.sbuf_tensor("cst128_sb", [128, 161], F32))
        cstb = _es.enter_context(nc.sbuf_tensor("cstb_sb", [128, 513], BF16))
        cst32 = _es.enter_context(nc.sbuf_tensor("cst32_sb", [32, 2048], BF16))
        cst1 = _es.enter_context(nc.sbuf_tensor("cst1_sb", [1, 1152], BF16))
        bvb_sb = _es.enter_context(nc.sbuf_tensor("bvb_sb", [128, D], BF16))
        idn_sb = cst128[:, 0:128]
        bq_sb = cst128[:, 128:136]
        bk_sb = cst128[:, 136:144]
        bpj_sb = cst128[:, 144:152]
        bfn_sb = cst128[:, 152:160]
        wqf_sb = cstb[:, 0:FDIM]
        wkf_sb = cstb[:, FDIM:2 * FDIM]
        oc_sb = cstb[:, 512:513]
        e32_sb = cst32[:, 0:NH * DH]
        ct_sb = cst32[:, NH * DH:2 * NH * DH]
        bv_sb = cst1[:, 0:D]
        or_sb = cst1[:, D:D + 128]
        xb0 = _es.enter_context(nc.sbuf_tensor("xb0", [128, NDC, T], F32))
        xb1 = _es.enter_context(nc.sbuf_tensor("xb1", [128, NDC, T], F32))
        xbf0 = _es.enter_context(nc.sbuf_tensor("xbf0", [128, T], BF16))
        xbf1 = _es.enter_context(nc.sbuf_tensor("xbf1", [128, T], BF16))
        sqr0 = _es.enter_context(nc.sbuf_tensor("sqr0", [128, T], BF16))
        sqr1 = _es.enter_context(nc.sbuf_tensor("sqr1", [128, T], BF16))
        rows_f = _es.enter_context(nc.sbuf_tensor("rows_f", [1, 3, T], F32))
        rows_b = _es.enter_context(nc.sbuf_tensor("rows_b", [1, 4, T], BF16))
        tmpA = _es.enter_context(nc.sbuf_tensor("tmpA", [128, T], F32))
        bsum_sb = _es.enter_context(nc.sbuf_tensor("bsum_sb", [128, 8], F32))
        xn_sb = _es.enter_context(nc.sbuf_tensor("xn_sb", [128, NDC, T], BF16))
        kt_sb = _es.enter_context(nc.sbuf_tensor("kt_sb", [128, NDC, T], BF16))
        vaug = _es.enter_context(nc.sbuf_tensor("vaug", [128, NTC, NH, 65], BF16))
        kp0 = _es.enter_context(nc.sbuf_tensor("kp0", [128, T], BF16))
        kp1 = _es.enter_context(nc.sbuf_tensor("kp1", [128, T], BF16))
        kp2 = _es.enter_context(nc.sbuf_tensor("kp2", [128, T], BF16))
        kp3 = _es.enter_context(nc.sbuf_tensor("kp3", [128, T], BF16))
        qps_all = [_es.enter_context(nc.sbuf_tensor(f"qp{i}", [128, T], BF16))
                   for i in range(6)]
        ctxacc = _es.enter_context(nc.sbuf_tensor("ctxacc", [128, NH * FDIM], F32))
        ctxF0 = _es.enter_context(nc.sbuf_tensor("ctxF0", [128, NH * 65], BF16))
        ctxF1 = _es.enter_context(nc.sbuf_tensor("ctxF1", [128, NH * 65], BF16))
        n16 = _es.enter_context(nc.sbuf_tensor("n16", [65, NH, T], BF16))
        D16 = _es.enter_context(nc.sbuf_tensor("D16", [100, T], F32))
        Di16 = _es.enter_context(nc.sbuf_tensor("Di16", [100, T], F32))
        HI16 = _es.enter_context(nc.sbuf_tensor("HI16", [100, T], BF16))
        LO16 = _es.enter_context(nc.sbuf_tensor("LO16", [100, T], BF16))
        HILO = _es.enter_context(nc.sbuf_tensor("HILO", [32, T], BF16))
        yT = _es.enter_context(nc.sbuf_tensor("yT", [128, NDC, T], BF16))
        yodd = _es.enter_context(nc.sbuf_tensor("yodd", [64, NDC, T], BF16))
        sT_sb = _es.enter_context(nc.sbuf_tensor("sT_sb", [32, T], BF16))
        osb0 = _es.enter_context(nc.sbuf_tensor("osb0", [128, T], F32))
        osb1 = _es.enter_context(nc.sbuf_tensor("osb1", [128, T], F32))
        pa = _es.enter_context(nc.psum_tensor("pa", [128, T], F32))
        pb = _es.enter_context(nc.psum_tensor("pb", [128, T], F32))
        pc = _es.enter_context(nc.psum_tensor("pc", [128, T], F32))
        pd = _es.enter_context(nc.psum_tensor("pd", [128, T], F32))
        prb = _es.enter_context(nc.psum_tensor("prb", [128, T], F32))
        pmrb = _es.enter_context(nc.psum_tensor("pmrb", [128, T], F32))
        px1 = _es.enter_context(nc.psum_tensor("px1", [128, T], F32))
        px2 = _es.enter_context(nc.psum_tensor("px2", [128, T], F32))
        s_t = _es.enter_context(nc.semaphore("s_t"))
        s_v = _es.enter_context(nc.semaphore("s_v"))
        s_s = _es.enter_context(nc.semaphore("s_s"))
        s_x = _es.enter_context(nc.semaphore("s_x"))
        s_g = _es.enter_context(nc.semaphore("s_g"))
        blk = _es.enter_context(nc.Block())

        rW1 = R(); rW2 = R(); rW3 = R(); rVr = R()
        rC128 = R(); rCB = R(); rC32 = R(); rC1 = R()
        rBvb = R()
        rXb = [[R() for _ in range(NDC)] for _ in range(2)]
        rXbf = [R(), R()]; rSqr = [R(), R()]
        rRowF = [R() for _ in range(4)]
        rRowB = [R() for _ in range(4)]
        rTmpA = R(); rBsum = R()
        rXn = [R() for _ in range(NDC)]
        rXnD = [[R() for _ in range(NDC)] for _ in range(NSC)]
        rKt = [R() for _ in range(NDC)]
        rVa = [[R(), R()] for _ in range(NTC)]
        rKp = [R(), R(), R(), R()]
        rQp = [R() for _ in range(6)]
        rCtx = R(); rCtxF = [R(), R()]
        rN16 = [R() for _ in range(NH)]
        rD16 = [R() for _ in range(4)]; rDi = R(); rHI = R(); rLO = R(); rHILO = [R() for _ in range(4)]
        rYe = [R() for _ in range(NDC)]
        rYo = [R() for _ in range(NDC)]
        rYoD = [R(), R()]
        rST = R()
        rOsb = [R(), R()]
        rOut = R()
        rPa = R(); rPb = R(); rPc = R(); rPd = R()
        rPrb = R(); rPmrb = R(); rPx1 = R(); rPx2 = R()

        def dma(eng, dst, src, reads, writes):
            plan.op(eng, lambda e, d=dst, s=src: e.dma_start(d, s), reads, writes, dma=True)

        dma("sync", cst128[:], d_cst128, [], [rC128])
        for dc in range(NDC):
            dma("gpsimd", W1[:, dc, :], d_wk[128 * dc:128 * (dc + 1), :], [], [rW1])
            dma("gpsimd", W2[:, dc, :], d_wv[128 * dc:128 * (dc + 1), :], [], [rW2])
            dma("gpsimd", vr_sb[:, dc, :], d_vr[128 * dc:128 * (dc + 1), :], [], [rVr])
        dma("gpsimd", cstb[:], d_cstb, [], [rCB])
        dma("gpsimd", cst32[:], d_cst32, [], [rC32])
        dma("gpsimd", cst1[:], d_cst1, [], [rC1])

        plan.op("vector", lambda e: e.memset(ctxacc[:], 0.0), [], [rCtx])
        plan.op("vector", lambda e: e.tensor_add(bsum_sb[:], bpj_sb, bfn_sb),
                [rC128], [rBsum])
        plan.op("vector", lambda e: e.memset(HILO[:], 0.0), [], list(rHILO))
        plan.op("vector", lambda e: e.memset(vaug[:, :, :, 64:65], 1.0), [rC1],
                [r for pair in rVa for r in pair])
        for half, ph, pr in ((0, px1, rPx1), (1, px2, rPx2)):
            plan.op("tensor",
                    lambda e, ph=ph, half=half: e.matmul(
                        ph[:], or_sb, bv_sb[0:1, 512 * half:512 * (half + 1)],
                        start=True, stop=True),
                    [rC1], [pr])
            plan.op("vector",
                    lambda e, ph=ph, half=half: e.tensor_copy(
                        bvb_sb[:, 512 * half:512 * (half + 1)], ph[:]),
                    [pr], [rBvb])


        def g_stats(s, pass2=False):
            xb, rXbc = (xb0, rXb[0]) if s % 2 == 0 else (xb1, rXb[1])
            yield ("dma",)
            for c in range(NDC):
                dma("sync", xb[:, c, :], xT[128 * c:128 * (c + 1), T * s:T * (s + 1)],
                    [], [rXbc[c]])
            if pass2:
                yield ("spacer",)
                return
            for cg in range(NDC // 2):
                yield ("stats", cg)
                plan.label = f"stats{s}"
                for c in (2 * cg, 2 * cg + 1):
                    xr, sr = (xbf0, sqr0) if c % 2 == 0 else (xbf1, sqr1)
                    rxr, rsr = rXbf[c % 2], rSqr[c % 2]
                    plan.op("scalar", lambda e, xr=xr, xb=xb, c=c: e.copy(
                        xr[:], xb[:, c, :]), [rXbc[c]], [rxr])
                    plan.op("vector", lambda e, sr=sr, xb=xb, c=c: e.tensor_mul(
                        sr[:], xb[:, c, :], xb[:, c, :]), [rXbc[c]], [rsr])
                    plan.op("tensor", lambda e, xr=xr, c=c: e.matmul(
                        prb[0:1, :], oc_sb, xr[:], start=(c == 0), stop=(c == NDC - 1)),
                        [rxr, rCB], [rPrb])
                    plan.op("tensor", lambda e, sr=sr, c=c: e.matmul(
                        pmrb[0:1, :], oc_sb, sr[:], start=(c == 0), stop=(c == NDC - 1)),
                        [rsr, rCB], [rPmrb])
            yield ("rows",)
            plan.label = f"rows{s}"
            mean = rows_f[0:1, 0, :]; slot1 = rows_f[0:1, 1, :]
            slot2 = rows_f[0:1, 2, :]
            plan.op("vector", lambda e: e.tensor_scalar_mul(mean, prb[0:1, :], 1.0 / D),
                    [rPrb], [rRowF[0]])
            plan.op("vector", lambda e: e.tensor_scalar(
                slot1, pmrb[0:1, :], 1.0 / D, 1e-5, ALU.mult, ALU.add),
                [rPmrb], [rRowF[1]])
            plan.op("vector", lambda e: e.tensor_mul(slot2, mean, mean),
                    [rRowF[0]], [rRowF[2]])
            plan.op("vector", lambda e: e.tensor_sub(slot1, slot1, slot2),
                    [rRowF[1], rRowF[2]], [rRowF[1]])
            plan.op("scalar", lambda e: e.activation(slot2, slot1, AF.Ln),
                    [rRowF[1]], [rRowF[2]])
            plan.op("scalar", lambda e: e.activation(slot1, slot2, AF.Exp, scale=-0.5),
                    [rRowF[2]], [rRowF[1]])
            plan.op("vector", lambda e: e.tensor_mul(slot2, mean, slot1),
                    [rRowF[0], rRowF[1]], [rRowF[2]])
            rstd = slot1; mr = slot2
            rhi = rows_b[0:1, 0, :]; rlo = rows_b[0:1, 1, :]
            mhi = rows_b[0:1, 2, :]; mlo = rows_b[0:1, 3, :]
            plan.op("vector", lambda e: e.tensor_copy(rhi, rstd), [rRowF[1]], [rRowB[0]])
            plan.op("vector", lambda e: e.tensor_sub(rlo, rstd, rhi),
                    [rRowF[1], rRowB[0]], [rRowB[1]])
            plan.op("vector", lambda e: e.tensor_copy(mhi, mr), [rRowF[2]], [rRowB[2]])
            plan.op("vector", lambda e: e.tensor_sub(mlo, mr, mhi),
                    [rRowF[2], rRowB[2]], [rRowB[3]])
            yield ("spacer",)
            yield ("bcast",)
            plan.label = f"bcast{s}"
            plan.op("tensor", lambda e: e.matmul(prb[:], or_sb, rhi, start=True, stop=False),
                    [rRowB[0], rC1], [rPrb])
            plan.op("tensor", lambda e: e.matmul(prb[:], or_sb, rlo, start=False, stop=True),
                    [rRowB[1], rC1], [rPrb])
            plan.op("tensor", lambda e: e.matmul(pmrb[:], or_sb, mhi, start=True, stop=False),
                    [rRowB[2], rC1], [rPmrb])
            plan.op("tensor", lambda e: e.matmul(pmrb[:], or_sb, mlo, start=False, stop=True),
                    [rRowB[3], rC1], [rPmrb])

        def g_apply(s):
            xb, rXbc = (xb0, rXb[0]) if s % 2 == 0 else (xb1, rXb[1])
            yield ("apply", 0)
            plan.label = f"apply{s}"
            for c in range(NDC):
                plan.op("vector", lambda e, xb=xb, c=c: e.tensor_mul(
                    tmpA[:], xb[:, c, :], prb[:]), [rXbc[c], rPrb], [rTmpA])
                plan.op("vector", lambda e, c=c: e.tensor_sub(
                    xn_sb[:, c, :], tmpA[:], pmrb[:]),
                    [rTmpA, rPmrb], [rXn[c]])
                dma("sync", d_xn[128 * c:128 * (c + 1), T * s:T * (s + 1)],
                    xn_sb[:, c, :], [rXn[c]], [rXnD[s][c]])

        def g_proj(proj_w, rProjW, proj_bias):
            for oc in range(NDC):
                ph, pr = (pa, rPa) if oc % 2 == 0 else (pb, rPb)
                for dc in range(NDC):
                    if dc % 2 == 0:
                        yield ("kT", oc)
                        plan.label = f"kT{oc}"
                    plan.op("tensor", lambda e, ph=ph, dc=dc, oc=oc: e.matmul(
                        ph[:], proj_w[:, dc, 128 * oc:128 * (oc + 1)], xn_sb[:, dc, :],
                        start=(dc == 0), stop=(dc == NDC - 1)),
                        [rProjW, rXn[dc]], [pr])
                plan.op("vector", lambda e, ph=ph, oc=oc: e.tensor_scalar(
                    kt_sb[:, oc, :], ph[:], proj_bias[:, oc:oc + 1], None, ALU.add),
                    [pr, rC128], [rKt[oc]])

        def g_v():
            for half in range(2):
                for tc in range(NTC):
                    ph, pr = (pa, rPa) if (tc + half) % 2 == 0 else (pb, rPb)
                    for dc in range(NDC):
                        if dc % 2 == 0:
                            yield ("v", half, tc)
                            plan.label = f"v{half}{tc}"
                        plan.op("tensor", lambda e, ph=ph, tc=tc, half=half, dc=dc: e.matmul(
                            ph[:], xn_sb[:, dc, 128 * tc:128 * (tc + 1)],
                            W2[:, dc, 512 * half:512 * (half + 1)],
                            start=(dc == 0), stop=(dc == NDC - 1)),
                            [rW2, rXn[dc]], [pr])
                    plan.op("vector", lambda e, ph=ph, tc=tc, half=half: e.tensor_add(
                        vaug[:, tc, 8 * half:8 * (half + 1), 0:64],
                        ph[:].rearrange("p (h x) -> p h x", h=8),
                        bvb_sb[:, 512 * half:512 * (half + 1)].rearrange(
                            "p (h x) -> p h x", h=8)),
                        [pr, rBvb], [rVa[tc][half]])

        def g_qt2(s):
            xb, rXbc = (xb0, rXb[0]) if s % 2 == 0 else (xb1, rXb[1])
            qbanks = ((pa, rPa), (pb, rPb), (px1, rPx1), (px2, rPx2))
            for ocb in range(2):
                for dc in range(NDC):
                    yield ("qt", ocb, dc)
                    for j in range(4):
                        oc = 4 * ocb + j
                        ph, pr = qbanks[j]
                        plan.op("tensor", lambda e, ph=ph, dc=dc, oc=oc: e.matmul(
                            ph[:], W1[:, dc, 128 * oc:128 * (oc + 1)], xn_sb[:, dc, :],
                            start=(dc == 0), stop=(dc == NDC - 1)),
                            [rW1, rXn[dc]], [pr])
                yield ("qtev", ocb)
                for j in range(4):
                    oc = 4 * ocb + j
                    ph, pr = qbanks[j]
                    plan.op("vector", lambda e, ph=ph, oc=oc: e.tensor_scalar(
                        kt_sb[:, oc, :], ph[:], bq_sb[:, oc:oc + 1], None, ALU.add),
                        [pr, rC128], [rKt[oc]])
            yield ("sT",)
            for dc in range(NDC):
                plan.op("tensor", lambda e, dc=dc: e.matmul(
                    pa[0:32, :], vr_sb[:, dc, :], xn_sb[:, dc, :],
                    start=(dc == 0), stop=(dc == NDC - 1)),
                    [rVr, rXn[dc]], [rPa])
            plan.op("vector", lambda e: e.tensor_copy(sT_sb[:], pa[0:32, :]),
                    [rPa], [rST])

        def g_sT():
            yield ("sT",)
            for dc in range(NDC):
                plan.op("tensor", lambda e, dc=dc: e.matmul(
                    pa[0:32, :], vr_sb[:, dc, :], xn_sb[:, dc, :],
                    start=(dc == 0), stop=(dc == NDC - 1)),
                    [rVr, rXn[dc]], [rPa])
            plan.op("vector", lambda e: e.tensor_copy(sT_sb[:], pa[0:32, :]),
                    [rPa], [rST])

        def _feat_ctx(h, tcg, kpx, rkpx):
            ctxp, rctxp = (px1, rPx1) if h % 2 == 0 else (px2, rPx2)
            for j in range(2):
                tc = 2 * tcg + j
                plan.op("tensor", lambda e, ctxp=ctxp, tc=tc, h=h, kpx=kpx, j=j: e.matmul(
                    ctxp[0:65, 0:FDIM], vaug[:, tc, h, :],
                    kpx[:, FDIM * j:FDIM * (j + 1)],
                    start=(tc == 0), stop=(tc == NTC - 1)),
                    [rVa[tc][h // 8], rkpx], [rctxp])
            if tcg == 1:
                plan.op("vector", lambda e, ctxp=ctxp, h=h: e.tensor_add(
                    ctxacc[0:65, FDIM * h:FDIM * (h + 1)],
                    ctxacc[0:65, FDIM * h:FDIM * (h + 1)], ctxp[0:65, 0:FDIM]),
                    [rctxp, rCtx], [rCtx])

        def g_feats():
            from collections import deque
            pend = deque()
            it = 0
            for h in range(NH):
                pr_i, hh = h // 2, h % 2
                for tcg in range(2):
                    yield ("feat", it)
                    plan.label = f"feat{it}"
                    kfp, rkfp = (pc, rPc) if it % 2 == 0 else (pd, rPd)
                    kpx, rkpx = ((kp0, rKp[0]), (kp1, rKp[1]), (kp2, rKp[2]), (kp3, rKp[3]))[it % 4]
                    for j in range(2):
                        tc = 2 * tcg + j
                        plan.op("tensor", lambda e, kfp=kfp, hh=hh, pr_i=pr_i, tc=tc, j=j: e.matmul(
                            kfp[:, FDIM * j:FDIM * (j + 1)],
                            kt_sb[64 * hh:64 * (hh + 1), pr_i, 128 * tc:128 * (tc + 1)],
                            wkf_sb[64 * hh:64 * (hh + 1), :],
                            start=True, stop=True),
                            [rKt[pr_i], rCB], [rkfp])
                    plan.op("scalar", lambda e, kpx=kpx, kfp=kfp: e.activation(
                        kpx[:], kfp[:], AF.Exp), [rkfp], [rkpx])
                    pend.append((h, tcg, kpx, rkpx))
                    if len(pend) > 2:
                        _feat_ctx(*pend.popleft())
                    it += 1
            while pend:
                yield ("featdrain", it)
                plan.label = f"fdrain{it}"
                it += 1
                _feat_ctx(*pend.popleft())

        def g_trans():
            yield ("wload",)
            for dc in range(NDC):
                dma("gpsimd", W1[:, dc, :], d_wq[128 * dc:128 * (dc + 1), :], [rW1], [rW1])
                dma("gpsimd", W2[:, dc, :], d_wproj[128 * dc:128 * (dc + 1), :], [rW2], [rW2])
                dma("gpsimd", W3[:, dc, :], d_wffnT[128 * dc:128 * (dc + 1), :], [rW3], [rW3])
            for h in range(NH):
                yield ("tp", h)
                for fc in range(2):
                    ph, pr = (pa, rPa) if fc == 0 else (pb, rPb)
                    ctxF, rcF = (ctxF0, rCtxF[0]) if fc == 0 else (ctxF1, rCtxF[1])
                    plan.op("tensor", lambda e, ph=ph, h=h, fc=fc: e.transpose(
                        ph[:, 0:128], ctxacc[:, FDIM * h + 128 * fc:FDIM * h + 128 * (fc + 1)],
                        idn_sb), [rCtx, rC128], [pr])
                    plan.op("vector", lambda e, ph=ph, ctxF=ctxF, h=h: e.tensor_copy(
                        ctxF[:, 65 * h:65 * (h + 1)], ph[:, 0:65]), [pr], [rcF])

        def ffn_part_t0():
            for dc in range(NDC):
                plan.op("tensor", lambda e, dc=dc: e.matmul(
                    px1[:], W3[:, dc, 256:384], xn_sb[:, dc, :],
                    start=(dc == 0), stop=False),
                    [rW3, rXn[dc]], [rPx1])
            plan.op("tensor", lambda e: e.matmul(
                px1[:], ct_sb[:, 256:384], sT_sb[:],
                start=False, stop=False),
                [rC32, rST], [rPx1])

        def heads_front(h):
            pr_i, hh = h // 2, h % 2
            banks = ((pc, rPc), (pd, rPd)) if h % 2 == 0 else ((px1, rPx1), (px2, rPx2))
            for fc in range(2):
                qfp, rqfp = banks[fc]
                qpx = qps_all[2 * (h % 3) + fc]
                rqpx = rQp[2 * (h % 3) + fc]
                plan.op("tensor", lambda e, qfp=qfp, hh=hh, fc=fc, pr_i=pr_i: e.matmul(
                    qfp[:], wqf_sb[64 * hh:64 * (hh + 1), 128 * fc:128 * (fc + 1)],
                    kt_sb[64 * hh:64 * (hh + 1), pr_i, :],
                    start=True, stop=True),
                    [rKt[pr_i], rCB], [rqfp])
                plan.op("scalar", lambda e, qpx=qpx, qfp=qfp: e.activation(
                    qpx[:], qfp[:], AF.Exp), [rqfp], [rqpx])

        def heads_numevac(h):
            nump, rnump = (prb, rPrb) if h % 2 == 0 else (pmrb, rPmrb)
            for fc in range(2):
                qpx = qps_all[2 * (h % 3) + fc]
                rqpx = rQp[2 * (h % 3) + fc]
                ctxF, rcF = (ctxF0, rCtxF[0]) if fc == 0 else (ctxF1, rCtxF[1])
                plan.op("tensor", lambda e, nump=nump, ctxF=ctxF, h=h, qpx=qpx, fc=fc: e.matmul(
                    nump[0:65, :], ctxF[:, 65 * h:65 * (h + 1)], qpx[:],
                    start=(fc == 0), stop=(fc == 1)),
                    [rcF, rqpx], [rnump])
            plan.op("vector", lambda e, nump=nump, h=h: e.tensor_copy(
                n16[:, h, :], nump[0:65, :]), [rnump], [rN16[h]])
            if h % 4 == 3:
                q4 = h // 4
                p0 = 32 * q4
                dma("gpsimd", D16[p0:p0 + 4, :],
                    n16[64:65, 4 * q4:4 * (q4 + 1), :],
                    [rN16[4 * q4], rN16[4 * q4 + 1], rN16[4 * q4 + 2], rN16[4 * q4 + 3]],
                    [rD16[q4]])

        def heads_rows(q4):
            sl = slice(32 * q4, 32 * q4 + 4)
            plan.op("vector", lambda e, sl=sl: e.tensor_scalar_add(
                Di16[sl, :], D16[sl, :], 1e-6), [rD16[q4]], [rDi])
            plan.op("scalar", lambda e, sl=sl: e.activation(
                D16[sl, :], Di16[sl, :], AF.Ln), [rDi], [rD16[q4]])
            plan.op("scalar", lambda e, sl=sl: e.activation(
                Di16[sl, :], D16[sl, :], AF.Exp, scale=-1.0), [rD16[q4]], [rDi])
            plan.op("vector", lambda e, sl=sl: e.tensor_copy(HI16[sl, :], Di16[sl, :]),
                    [rDi], [rHI])
            plan.op("vector", lambda e, sl=sl: e.tensor_sub(
                LO16[sl, :], Di16[sl, :], HI16[sl, :]), [rDi, rHI], [rLO])
            a = 4 * q4
            dma("gpsimd", HILO[a:a + 4, :], HI16[sl, :], [rHI], [rHILO[q4]])
            dma("gpsimd", HILO[16 + a:16 + a + 4, :], LO16[sl, :], [rLO], [rHILO[q4]])

        def warm_mm(n, bank, rbank):
            for _ in range(n):
                plan.op("tensor", lambda e, bank=bank: e.matmul(
                    bank[0:1, :], oc_sb, W1[:, 0, 0:T], start=True, stop=True),
                    [rW1, rCB], [rbank])

        def heads_sel_mult(h):
            dbp, rdbp = (prb, rPrb) if h % 2 == 0 else (pmrb, rPmrb)
            plan.op("tensor", lambda e, dbp=dbp, h=h: e.matmul(
                dbp[0:64, :], e32_sb[:, 64 * h:64 * (h + 1)], HILO[:],
                start=True, stop=True), [rC32, rHILO[h // 4]], [rdbp])
            oc = h // 2
            if h % 2 == 0:
                plan.op("vector", lambda e, dbp=dbp, h=h, oc=oc: e.tensor_mul(
                    yT[0:64, oc, :], n16[0:64, h, :], dbp[0:64, :]),
                    [rN16[h], rdbp], [rYe[oc]])
            else:
                plan.op("vector", lambda e, dbp=dbp, h=h, oc=oc: e.tensor_mul(
                    yodd[:, oc, :], n16[0:64, h, :], dbp[0:64, :]),
                    [rN16[h], rdbp], [rYo[oc]])

        def g_heads():
            for h in range(NH):
                yield ("head", h)
                heads_front(h)
                if 4 <= h < 12:
                    oc = (h - 4) // 4
                    ph, pr = (pa, rPa) if oc == 0 else (pb, rPb)
                    j0 = ((h - 4) % 4) * 2
                    for dc in (j0, j0 + 1):
                        if dc < NDC:
                            plan.op("tensor", lambda e, ph=ph, dc=dc, oc=oc: e.matmul(
                                ph[:], W3[:, dc, 128 * oc:128 * (oc + 1)], xn_sb[:, dc, :],
                                start=(dc == 0), stop=False),
                                [rW3, rXn[dc]], [pr])
                    if (h - 4) % 4 == 3:
                        plan.op("tensor", lambda e, ph=ph, oc=oc: e.matmul(
                            ph[:], ct_sb[:, 128 * oc:128 * (oc + 1)], sT_sb[:],
                            start=False, stop=False),
                            [rC32, rST], [pr])
                if h >= 2:
                    heads_numevac(h - 2)
                if h == NH - 1:
                    heads_numevac(h - 1)
                if h >= 5 and h % 4 == 1:
                    heads_rows((h - 5) // 4)
            yield ("t0",)
            heads_numevac(15)
            heads_rows(3)

        def g_out(s, st=None, s_next=None):
            xb, rXbc = (xb0, rXb[0]) if s % 2 == 0 else (xb1, rXb[1])
            obank = (pa, pb, px1, px2, pc, pd)
            orbank = (rPa, rPb, rPx1, rPx2, rPc, rPd)

            def ffn_part(oc, ph, pr):
                for dc in range(NDC):
                    plan.op("tensor", lambda e, ph=ph, dc=dc, oc=oc: e.matmul(
                        ph[:], W3[:, dc, 128 * oc:128 * (oc + 1)], xn_sb[:, dc, :],
                        start=(dc == 0), stop=False),
                        [rW3, rXn[dc]], [pr])
                plan.op("tensor", lambda e, ph=ph, oc=oc: e.matmul(
                    ph[:], ct_sb[:, 128 * oc:128 * (oc + 1)], sT_sb[:],
                    start=False, stop=False),
                    [rC32, rST], [pr])

            def proj_part(oc, ph, pr):
                for hc in range(NDC):
                    plan.op("tensor", lambda e, ph=ph, hc=hc, oc=oc: e.matmul(
                        ph[:], W2[:, hc, 128 * oc:128 * (oc + 1)], yT[:, hc, :],
                        start=False, stop=(hc == NDC - 1)),
                        [rW2, rYe[hc], rYoD[hc // 4]], [pr])

            def evac(oc, ph, pr):
                osb, rosb = (osb0, rOsb[0]) if oc % 2 == 0 else (osb1, rOsb[1])
                plan.op("vector", lambda e, ph=ph, osb=osb, xb=xb, oc=oc: e.scalar_tensor_tensor(
                    osb[:], ph[:], bsum_sb[:, oc:oc + 1], xb[:, oc, :],
                    ALU.add, ALU.add), [pr, rBsum, rXbc[oc]], [rosb])
                dma("sync", outT[128 * oc:128 * (oc + 1), T * s:T * (s + 1)], osb[:],
                    [rosb], [rOut] if (s == NSC - 1 and oc == 7) else [])

            for oc in range(2, 6):
                yield ("outA", oc)
                if oc >= 3 and st is not None and st.alive:
                    st.step()
                ffn_part(oc, obank[oc], orbank[oc])
                if oc == 3:
                    for h in range(NH):
                        heads_sel_mult(h)
                        if h == 7:
                            dma("gpsimd", yT[64:128, 0:4, :], yodd[:, 0:4, :],
                                [rYo[0], rYo[1], rYo[2], rYo[3]], [rYoD[0]])
                    dma("gpsimd", yT[64:128, 4:8, :], yodd[:, 4:8, :],
                        [rYo[4], rYo[5], rYo[6], rYo[7]], [rYoD[1]])
            for oc in range(6):
                yield ("outB", oc)
                if st is not None and st.alive:
                    st.step()
                proj_part(oc, obank[oc], orbank[oc])
                evac(oc, obank[oc], orbank[oc])
                if oc == 2:
                    yield ("outCf", 6)
                    if st is not None and st.alive:
                        st.step()
                    ffn_part(6, pa, rPa)
                if oc == 3:
                    yield ("outCf", 7)
                    if st is not None and st.alive:
                        st.step()
                    ffn_part(7, pb, rPb)
                    if s_next is not None:
                        for c in range(NDC):
                            dma("gpsimd", xn_sb[:, c, :],
                                d_xn[128 * c:128 * (c + 1), T * s_next:T * (s_next + 1)],
                                [rXnD[s_next][c]], [rXn[c]])
            for oc in (6, 7):
                yield ("outC", oc)
                if st is not None and st.alive:
                    st.step()
                ph, pr = (pa, rPa) if oc == 6 else (pb, rPb)
                proj_part(oc, ph, pr)
                evac(oc, ph, pr)

        class Stepper:
            def __init__(self, gen):
                self.gen = gen
                self.alive = True
                self.tag = None
                self._adv()

            def _adv(self):
                try:
                    self.tag = next(self.gen)
                except StopIteration:
                    self.alive = False
                    self.tag = None

            def step(self):
                self._adv()

        def chain(*gens):
            for g in gens:
                for t in g:
                    yield t

        def drain(gen):
            for _ in gen:
                pass

        def drive(lgen, fgen, lready, wl, wf):
            l = Stepper(lgen); f = Stepper(fgen)
            cl = cf = 0.0
            fprog = 0
            while l.alive or f.alive:
                want_l = (l.alive and (not f.alive or cl * wf <= cf * wl)
                          and lready(l.tag, fprog, f.alive))
                if want_l:
                    l.step(); cl += 1
                elif f.alive:
                    f.step(); fprog += 1; cf += 1
                elif l.alive:
                    l.step(); cl += 1

        def p1_ready(tag, fprog, falive):
            if not falive:
                return True
            if tag[0] == "kT":
                return fprog >= 4 * tag[1] + 4
            if tag[0] == "v":
                return fprog >= (18, 34)[tag[1]]
            if tag[0] == "bcast":
                return fprog >= 8
            return True

        def tr_ready(tag, fprog, falive):
            if not falive:
                return True
            if tag[0] == "tp":
                return fprog >= min(34, 2 * tag[1] + 4)
            return True

        drain(chain(g_stats(0), g_apply(0), g_proj(W1, rW1, bk_sb), g_v()))
        for s in range(1, NSC):
            drive(chain(g_stats(s), g_apply(s), g_proj(W1, rW1, bk_sb), g_v()),
                  g_feats(), p1_ready, 71, 34)
        drive(g_trans(), g_feats(), tr_ready, 17, 32)
        drain(g_stats(0, pass2=True))
        for c in range(NDC):
            dma("gpsimd", xn_sb[:, c, :], d_xn[128 * c:128 * (c + 1), 0:T],
                [rXnD[0][c]], [rXn[c]])
        drain(g_qt2(0))
        for s in range(NSC):
            drain(g_heads())
            st = Stepper(g_stats(s + 1, pass2=True)) if s < NSC - 1 else None
            drain(g_out(s, st, s + 1 if s < NSC - 1 else None))
            if s < NSC - 1:
                while st.alive:
                    st.step()
                drain(g_qt2(s + 1))

        plan.op("sync", None, [rOut], [])

        plan.finalize()

        sem_of = {"tensor": s_t, "vector": s_v, "scalar": s_s,
                  "sync": s_x, "gpsimd": s_g}

        def resolve(i):
            o = plan.ops[i]
            if o.dma:
                return (o.eng, 16 * o.dmarank)
            return (o.eng, o.semval)

        def emit(engine_name, eng):
            hw = {}
            for o in plan.ops:
                if o.eng != engine_name:
                    continue
                targets = {}
                for wi in o.waits:
                    sname, val = resolve(wi)
                    if val is None:
                        continue
                    if sname == engine_name and not plan.ops[wi].dma:
                        continue
                    if val > targets.get(sname, -1):
                        targets[sname] = val
                for sname, val in sorted(targets.items()):
                    if val > hw.get(sname, 0):
                        eng.wait_ge(sem_of[sname], val)
                        hw[sname] = val
                if o.fn is None:
                    continue
                inst = o.fn(eng)
                if o.dma:
                    inst.then_inc(sem_of[o.eng], 16)
                elif o.mark:
                    inst.then_inc(sem_of[o.eng], 1)

        @blk.sync
        def _(sync):
            emit("sync", sync)

        @blk.gpsimd
        def _(gp):
            emit("gpsimd", gp)

        @blk.tensor
        def _(tn):
            emit("tensor", tn)

        @blk.vector
        def _(vc):
            emit("vector", vc)

        @blk.scalar
        def _(sc):
            emit("scalar", sc)

    return nc


_NC_CACHE = None


def kernel(L0_attn_x, L0_attn_Wq, L0_attn_bq, L0_attn_Wk, L0_attn_bk,
           L0_attn_Wv, L0_attn_bv, L0_attn_Wproj, L0_attn_bproj,
           L0_attn_wq_feat, L0_attn_wk_feat, L0_ffn_W, L0_ffn_b,
           L0_ffn_V_r, L0_ffn_C):
    global _NC_CACHE
    x = np.asarray(L0_attn_x, dtype=np.float32)
    f32 = lambda a: np.ascontiguousarray(np.asarray(a, dtype=np.float32))

    wq = f32(L0_attn_Wq); wk = f32(L0_attn_Wk); wv = f32(L0_attn_Wv)
    wproj = f32(L0_attn_Wproj)
    wffnT = np.ascontiguousarray(f32(L0_ffn_W).T)
    vr = f32(L0_ffn_V_r)
    ct = np.ascontiguousarray(f32(L0_ffn_C).T)
    wqf2 = np.ascontiguousarray(np.vstack([f32(L0_attn_wq_feat)] * 2))
    wkf2 = np.ascontiguousarray(np.vstack([f32(L0_attn_wk_feat)] * 2))

    def b2(v):
        return np.ascontiguousarray(f32(v).reshape(8, 128).T)

    bq2 = b2(L0_attn_bq); bk2 = b2(L0_attn_bk)
    bpj2 = b2(L0_attn_bproj); bfn2 = b2(L0_ffn_b)
    bv1 = f32(L0_attn_bv).reshape(1, D)

    e32 = np.zeros((32, NH * DH), np.float32)
    for h in range(NH):
        e32[h, DH * h:DH * (h + 1)] = 1.0
        e32[16 + h, DH * h:DH * (h + 1)] = 1.0
    cst128 = np.zeros((128, 161), np.float32)
    cst128[:, 0:128] = np.eye(128, dtype=np.float32)
    cst128[:, 128:136] = bq2
    cst128[:, 136:144] = bk2
    cst128[:, 144:152] = bpj2
    cst128[:, 152:160] = bfn2
    cst128[:, 160] = 1.0
    cstb = np.zeros((128, 513), np.float32)
    cstb[:, 0:256] = wqf2
    cstb[:, 256:512] = wkf2
    cstb[:, 512] = 1.0
    cst32 = np.zeros((32, 2048), np.float32)
    cst32[:, 0:1024] = e32
    cst32[:, 1024:2048] = ct
    cst1 = np.zeros((1, 1152), np.float32)
    cst1[0, 0:1024] = bv1[0]
    cst1[0, 1024:1152] = 1.0

    if _NC_CACHE is None:
        _NC_CACHE = build_nc()
    nc = _NC_CACHE

    shared = {
        "wq": wq, "wk": wk, "wv": wv, "wproj": wproj, "wffnT": wffnT,
        "vr": vr, "cst128": cst128, "cstb": cstb, "cst32": cst32, "cst1": cst1,
    }
    in_maps = []
    for b in range(8):
        m = dict(shared)
        m["xT"] = np.ascontiguousarray(x[b].T)
        in_maps.append(m)

    res = run_bass_kernel_spmd(nc, in_maps, core_ids=list(range(8)))
    out = np.empty((8, NTOK, D), np.float32)
    for b in range(8):
        out[b] = res.results[b]["outT"].T
    return out


# version 33
# speedup vs baseline: 1.0168x; 1.0095x over previous
import numpy as np
import concourse.bass as bass
import concourse.mybir as mybir
from concourse.bass_utils import run_bass_kernel_spmd

F32 = mybir.dt.float32
BF16 = mybir.dt.bfloat16
AF = mybir.ActivationFunctionType
ALU = mybir.AluOpType

D = 1024
NTOK = 8192
NSC = 16
T = 512
NH = 16
DH = 64
FDIM = 256
NDC = 8
NTC = 4


class _Res:
    __slots__ = ("w", "readers")

    def __init__(self):
        self.w = None
        self.readers = []


class _Op:
    __slots__ = ("eng", "fn", "waits", "dma", "mark", "semval", "dmarank", "label")

    def __init__(self, eng, fn, waits, dma):
        self.eng = eng
        self.fn = fn
        self.waits = waits
        self.dma = dma
        self.mark = False
        self.semval = None
        self.dmarank = None


class _Plan:
    def __init__(self):
        self.ops = []
        self.label = ""

    def op(self, eng, fn, reads=(), writes=(), dma=False):
        i = len(self.ops)
        waits = set()
        for r in reads:
            if r.w is not None:
                wo = self.ops[r.w]
                if dma or wo.dma or wo.eng != eng:
                    waits.add(r.w)
                    wo.mark = True
            r.readers.append(i)
        for r in writes:
            deps = ([] if r.w is None else [r.w]) + r.readers
            for d in deps:
                if d == i:
                    continue
                do = self.ops[d]
                if dma or do.dma or do.eng != eng:
                    waits.add(d)
                    do.mark = True
            r.w = i
            r.readers = []
        o = _Op(eng, fn, sorted(waits), dma)
        o.label = self.label
        self.ops.append(o)
        return i

    def finalize(self):
        cnt = {"tensor": 0, "vector": 0, "scalar": 0}
        dcnt = {"sync": 0, "gpsimd": 0, "scalar_dma": 0}
        for o in self.ops:
            if o.dma:
                dcnt[o.eng if o.eng != "scalar" else "scalar_dma"] += 1
                o.dmarank = dcnt[o.eng if o.eng != "scalar" else "scalar_dma"]
                o.mark = True
            elif o.eng in cnt:
                if o.mark:
                    cnt[o.eng] += 1
                    o.semval = cnt[o.eng]


def build_nc():
    nc = bass.Bass("TRN2", target_bir_lowering=False)

    xT = nc.dram_tensor("xT", [D, NTOK], F32, kind="ExternalInput").ap()
    outT = nc.dram_tensor("outT", [D, NTOK], F32, kind="ExternalOutput").ap()
    d_wq = nc.dram_tensor("wq", [D, D], F32, kind="ExternalInput").ap()
    d_wk = nc.dram_tensor("wk", [D, D], F32, kind="ExternalInput").ap()
    d_wv = nc.dram_tensor("wv", [D, D], F32, kind="ExternalInput").ap()
    d_wproj = nc.dram_tensor("wproj", [D, D], F32, kind="ExternalInput").ap()
    d_wffnT = nc.dram_tensor("wffnT", [D, D], F32, kind="ExternalInput").ap()
    d_vr = nc.dram_tensor("vr", [D, 32], F32, kind="ExternalInput").ap()
    d_cst128 = nc.dram_tensor("cst128", [128, 161], F32, kind="ExternalInput").ap()
    d_cstb = nc.dram_tensor("cstb", [128, 513], F32, kind="ExternalInput").ap()
    d_cst32 = nc.dram_tensor("cst32", [32, 2048], F32, kind="ExternalInput").ap()
    d_cst1 = nc.dram_tensor("cst1", [1, 1152], F32, kind="ExternalInput").ap()
    d_rows = nc.dram_tensor("rows_scratch", [NSC, 4, T], BF16, kind="Internal").ap()
    d_xn = nc.dram_tensor("xn_scratch", [D, NTOK], BF16, kind="Internal").ap()

    plan = _Plan()
    R = _Res

    import contextlib
    _es = contextlib.ExitStack()
    with _es:
        W1 = _es.enter_context(nc.sbuf_tensor("W1", [128, NDC, D], BF16))
        W2 = _es.enter_context(nc.sbuf_tensor("W2", [128, NDC, D], BF16))
        W3 = _es.enter_context(nc.sbuf_tensor("W3", [128, NDC, D], BF16))
        vr_sb = _es.enter_context(nc.sbuf_tensor("vr_sb", [128, NDC, 32], BF16))
        cst128 = _es.enter_context(nc.sbuf_tensor("cst128_sb", [128, 161], F32))
        cstb = _es.enter_context(nc.sbuf_tensor("cstb_sb", [128, 513], BF16))
        cst32 = _es.enter_context(nc.sbuf_tensor("cst32_sb", [32, 2048], BF16))
        cst1 = _es.enter_context(nc.sbuf_tensor("cst1_sb", [1, 1152], BF16))
        bvb_sb = _es.enter_context(nc.sbuf_tensor("bvb_sb", [128, D], BF16))
        idn_sb = cst128[:, 0:128]
        bq_sb = cst128[:, 128:136]
        bk_sb = cst128[:, 136:144]
        bpj_sb = cst128[:, 144:152]
        bfn_sb = cst128[:, 152:160]
        wqf_sb = cstb[:, 0:FDIM]
        wkf_sb = cstb[:, FDIM:2 * FDIM]
        oc_sb = cstb[:, 512:513]
        e32_sb = cst32[:, 0:NH * DH]
        ct_sb = cst32[:, NH * DH:2 * NH * DH]
        bv_sb = cst1[:, 0:D]
        or_sb = cst1[:, D:D + 128]
        xb0 = _es.enter_context(nc.sbuf_tensor("xb0", [128, NDC, T], F32))
        xb1 = _es.enter_context(nc.sbuf_tensor("xb1", [128, NDC, T], F32))
        xbf0 = _es.enter_context(nc.sbuf_tensor("xbf0", [128, T], BF16))
        xbf1 = _es.enter_context(nc.sbuf_tensor("xbf1", [128, T], BF16))
        sqr0 = _es.enter_context(nc.sbuf_tensor("sqr0", [128, T], BF16))
        sqr1 = _es.enter_context(nc.sbuf_tensor("sqr1", [128, T], BF16))
        rows_f = _es.enter_context(nc.sbuf_tensor("rows_f", [1, 3, T], F32))
        rows_b = _es.enter_context(nc.sbuf_tensor("rows_b", [1, 4, T], BF16))
        tmpA = _es.enter_context(nc.sbuf_tensor("tmpA", [128, T], F32))
        bsum_sb = _es.enter_context(nc.sbuf_tensor("bsum_sb", [128, 8], F32))
        xn_sb = _es.enter_context(nc.sbuf_tensor("xn_sb", [128, NDC, T], BF16))
        kt_sb = _es.enter_context(nc.sbuf_tensor("kt_sb", [128, NDC, T], BF16))
        vaug = _es.enter_context(nc.sbuf_tensor("vaug", [128, NTC, NH, 65], BF16))
        kp0 = _es.enter_context(nc.sbuf_tensor("kp0", [128, T], BF16))
        kp1 = _es.enter_context(nc.sbuf_tensor("kp1", [128, T], BF16))
        kp2 = _es.enter_context(nc.sbuf_tensor("kp2", [128, T], BF16))
        kp3 = _es.enter_context(nc.sbuf_tensor("kp3", [128, T], BF16))
        qps_all = [_es.enter_context(nc.sbuf_tensor(f"qp{i}", [128, T], BF16))
                   for i in range(6)]
        ctxacc = _es.enter_context(nc.sbuf_tensor("ctxacc", [128, NH * FDIM], F32))
        ctxF0 = _es.enter_context(nc.sbuf_tensor("ctxF0", [128, NH * 65], BF16))
        ctxF1 = _es.enter_context(nc.sbuf_tensor("ctxF1", [128, NH * 65], BF16))
        n16 = _es.enter_context(nc.sbuf_tensor("n16", [65, NH, T], BF16))
        D16 = _es.enter_context(nc.sbuf_tensor("D16", [100, T], F32))
        Di16 = _es.enter_context(nc.sbuf_tensor("Di16", [100, T], F32))
        HI16 = _es.enter_context(nc.sbuf_tensor("HI16", [100, T], BF16))
        LO16 = _es.enter_context(nc.sbuf_tensor("LO16", [100, T], BF16))
        HILO = _es.enter_context(nc.sbuf_tensor("HILO", [32, T], BF16))
        yT = _es.enter_context(nc.sbuf_tensor("yT", [128, NDC, T], BF16))
        yodd = _es.enter_context(nc.sbuf_tensor("yodd", [64, NDC, T], BF16))
        sT_sb = _es.enter_context(nc.sbuf_tensor("sT_sb", [32, T], BF16))
        osb0 = _es.enter_context(nc.sbuf_tensor("osb0", [128, T], F32))
        osb1 = _es.enter_context(nc.sbuf_tensor("osb1", [128, T], F32))
        pa = _es.enter_context(nc.psum_tensor("pa", [128, T], F32))
        pb = _es.enter_context(nc.psum_tensor("pb", [128, T], F32))
        pc = _es.enter_context(nc.psum_tensor("pc", [128, T], F32))
        pd = _es.enter_context(nc.psum_tensor("pd", [128, T], F32))
        prb = _es.enter_context(nc.psum_tensor("prb", [128, T], F32))
        pmrb = _es.enter_context(nc.psum_tensor("pmrb", [128, T], F32))
        px1 = _es.enter_context(nc.psum_tensor("px1", [128, T], F32))
        px2 = _es.enter_context(nc.psum_tensor("px2", [128, T], F32))
        s_t = _es.enter_context(nc.semaphore("s_t"))
        s_v = _es.enter_context(nc.semaphore("s_v"))
        s_s = _es.enter_context(nc.semaphore("s_s"))
        s_x = _es.enter_context(nc.semaphore("s_x"))
        s_g = _es.enter_context(nc.semaphore("s_g"))
        blk = _es.enter_context(nc.Block())

        rW1 = [R() for _ in range(NDC)]; rW2 = [R() for _ in range(NDC)]
        rW3 = [R() for _ in range(NDC)]; rVr = R()
        rC128 = R(); rCB = R(); rC32 = R(); rC1 = R()
        rBvb = R()
        rXb = [[R() for _ in range(NDC)] for _ in range(2)]
        rXbf = [R(), R()]; rSqr = [R(), R()]
        rRowF = [R() for _ in range(4)]
        rRowB = [R() for _ in range(4)]
        rTmpA = R(); rBsum = R()
        rXn = [R() for _ in range(NDC)]
        rXnD = [[R() for _ in range(NDC)] for _ in range(NSC)]
        rKt = [R() for _ in range(NDC)]
        rVa = [[R(), R()] for _ in range(NTC)]
        rKp = [R(), R(), R(), R()]
        rQp = [R() for _ in range(6)]
        rCtx = R(); rCtxF = [R(), R()]
        rN16 = [R() for _ in range(NH)]
        rD16 = [R() for _ in range(4)]; rDi = R(); rHI = R(); rLO = R(); rHILO = [(R(), R()) for _ in range(4)]
        rYe = [R() for _ in range(NDC)]
        rYo = [R() for _ in range(NDC)]
        rYoD = [R(), R()]
        rST = R()
        rOsb = [R(), R()]
        rOut = R()
        rPa = R(); rPb = R(); rPc = R(); rPd = R()
        rPrb = R(); rPmrb = R(); rPx1 = R(); rPx2 = R()

        def dma(eng, dst, src, reads, writes):
            plan.op(eng, lambda e, d=dst, s=src: e.dma_start(d, s), reads, writes, dma=True)

        dma("sync", cst128[:], d_cst128, [], [rC128])
        for dc in range(NDC):
            dma("gpsimd", W1[:, dc, :], d_wk[128 * dc:128 * (dc + 1), :], [], [rW1[dc]])
            dma("gpsimd", W2[:, dc, :], d_wv[128 * dc:128 * (dc + 1), :], [], [rW2[dc]])
            dma("gpsimd", vr_sb[:, dc, :], d_vr[128 * dc:128 * (dc + 1), :], [], [rVr])
        dma("gpsimd", cstb[:], d_cstb, [], [rCB])
        dma("gpsimd", cst32[:], d_cst32, [], [rC32])
        dma("gpsimd", cst1[:], d_cst1, [], [rC1])

        plan.op("vector", lambda e: e.memset(ctxacc[:], 0.0), [], [rCtx])
        plan.op("vector", lambda e: e.tensor_add(bsum_sb[:], bpj_sb, bfn_sb),
                [rC128], [rBsum])
        plan.op("vector", lambda e: e.memset(HILO[:], 0.0), [], [r for pr_ in rHILO for r in pr_])
        plan.op("vector", lambda e: e.memset(vaug[:, :, :, 64:65], 1.0), [rC1],
                [r for pair in rVa for r in pair])
        for half, ph, pr in ((0, px1, rPx1), (1, px2, rPx2)):
            plan.op("tensor",
                    lambda e, ph=ph, half=half: e.matmul(
                        ph[:], or_sb, bv_sb[0:1, 512 * half:512 * (half + 1)],
                        start=True, stop=True),
                    [rC1], [pr])
            plan.op("vector",
                    lambda e, ph=ph, half=half: e.tensor_copy(
                        bvb_sb[:, 512 * half:512 * (half + 1)], ph[:]),
                    [pr], [rBvb])


        def g_stats(s, pass2=False):
            xb, rXbc = (xb0, rXb[0]) if s % 2 == 0 else (xb1, rXb[1])
            yield ("dma",)
            for c in range(NDC):
                dma("sync", xb[:, c, :], xT[128 * c:128 * (c + 1), T * s:T * (s + 1)],
                    [], [rXbc[c]])
            if pass2:
                yield ("spacer",)
                return
            for cg in range(NDC // 2):
                yield ("stats", cg)
                plan.label = f"stats{s}"
                for c in (2 * cg, 2 * cg + 1):
                    xr, sr = (xbf0, sqr0) if c % 2 == 0 else (xbf1, sqr1)
                    rxr, rsr = rXbf[c % 2], rSqr[c % 2]
                    plan.op("scalar", lambda e, xr=xr, xb=xb, c=c: e.copy(
                        xr[:], xb[:, c, :]), [rXbc[c]], [rxr])
                    plan.op("vector", lambda e, sr=sr, xb=xb, c=c: e.tensor_mul(
                        sr[:], xb[:, c, :], xb[:, c, :]), [rXbc[c]], [rsr])
                    plan.op("tensor", lambda e, xr=xr, c=c: e.matmul(
                        prb[0:1, :], oc_sb, xr[:], start=(c == 0), stop=(c == NDC - 1)),
                        [rxr, rCB], [rPrb])
                    plan.op("tensor", lambda e, sr=sr, c=c: e.matmul(
                        pmrb[0:1, :], oc_sb, sr[:], start=(c == 0), stop=(c == NDC - 1)),
                        [rsr, rCB], [rPmrb])
            yield ("rows",)
            plan.label = f"rows{s}"
            mean = rows_f[0:1, 0, :]; slot1 = rows_f[0:1, 1, :]
            slot2 = rows_f[0:1, 2, :]
            plan.op("vector", lambda e: e.tensor_scalar_mul(mean, prb[0:1, :], 1.0 / D),
                    [rPrb], [rRowF[0]])
            plan.op("vector", lambda e: e.tensor_scalar(
                slot1, pmrb[0:1, :], 1.0 / D, 1e-5, ALU.mult, ALU.add),
                [rPmrb], [rRowF[1]])
            plan.op("vector", lambda e: e.tensor_mul(slot2, mean, mean),
                    [rRowF[0]], [rRowF[2]])
            plan.op("vector", lambda e: e.tensor_sub(slot1, slot1, slot2),
                    [rRowF[1], rRowF[2]], [rRowF[1]])
            plan.op("scalar", lambda e: e.activation(slot2, slot1, AF.Ln),
                    [rRowF[1]], [rRowF[2]])
            plan.op("scalar", lambda e: e.activation(slot1, slot2, AF.Exp, scale=-0.5),
                    [rRowF[2]], [rRowF[1]])
            plan.op("vector", lambda e: e.tensor_mul(slot2, mean, slot1),
                    [rRowF[0], rRowF[1]], [rRowF[2]])
            rstd = slot1; mr = slot2
            rhi = rows_b[0:1, 0, :]; rlo = rows_b[0:1, 1, :]
            mhi = rows_b[0:1, 2, :]; mlo = rows_b[0:1, 3, :]
            plan.op("vector", lambda e: e.tensor_copy(rhi, rstd), [rRowF[1]], [rRowB[0]])
            plan.op("vector", lambda e: e.tensor_sub(rlo, rstd, rhi),
                    [rRowF[1], rRowB[0]], [rRowB[1]])
            plan.op("vector", lambda e: e.tensor_copy(mhi, mr), [rRowF[2]], [rRowB[2]])
            plan.op("vector", lambda e: e.tensor_sub(mlo, mr, mhi),
                    [rRowF[2], rRowB[2]], [rRowB[3]])
            yield ("spacer",)
            yield ("bcast",)
            plan.label = f"bcast{s}"
            plan.op("tensor", lambda e: e.matmul(prb[:], or_sb, rhi, start=True, stop=False),
                    [rRowB[0], rC1], [rPrb])
            plan.op("tensor", lambda e: e.matmul(prb[:], or_sb, rlo, start=False, stop=True),
                    [rRowB[1], rC1], [rPrb])
            plan.op("tensor", lambda e: e.matmul(pmrb[:], or_sb, mhi, start=True, stop=False),
                    [rRowB[2], rC1], [rPmrb])
            plan.op("tensor", lambda e: e.matmul(pmrb[:], or_sb, mlo, start=False, stop=True),
                    [rRowB[3], rC1], [rPmrb])

        def g_apply(s):
            xb, rXbc = (xb0, rXb[0]) if s % 2 == 0 else (xb1, rXb[1])
            yield ("apply", 0)
            plan.label = f"apply{s}"
            for c in range(NDC):
                plan.op("vector", lambda e, xb=xb, c=c: e.tensor_mul(
                    tmpA[:], xb[:, c, :], prb[:]), [rXbc[c], rPrb], [rTmpA])
                plan.op("vector", lambda e, c=c: e.tensor_sub(
                    xn_sb[:, c, :], tmpA[:], pmrb[:]),
                    [rTmpA, rPmrb], [rXn[c]])
                dma("sync", d_xn[128 * c:128 * (c + 1), T * s:T * (s + 1)],
                    xn_sb[:, c, :], [rXn[c]], [rXnD[s][c]])

        def g_proj(proj_w, rProjW, proj_bias):
            for oc in range(NDC):
                ph, pr = (pa, rPa) if oc % 2 == 0 else (pb, rPb)
                for dc in range(NDC):
                    if dc % 2 == 0:
                        yield ("kT", oc)
                        plan.label = f"kT{oc}"
                    plan.op("tensor", lambda e, ph=ph, dc=dc, oc=oc: e.matmul(
                        ph[:], proj_w[:, dc, 128 * oc:128 * (oc + 1)], xn_sb[:, dc, :],
                        start=(dc == 0), stop=(dc == NDC - 1)),
                        [rProjW[dc], rXn[dc]], [pr])
                plan.op("vector", lambda e, ph=ph, oc=oc: e.tensor_scalar(
                    kt_sb[:, oc, :], ph[:], proj_bias[:, oc:oc + 1], None, ALU.add),
                    [pr, rC128], [rKt[oc]])

        def g_v():
            for half in range(2):
                for tc in range(NTC):
                    ph, pr = (pa, rPa) if (tc + half) % 2 == 0 else (pb, rPb)
                    for dc in range(NDC):
                        if dc % 2 == 0:
                            yield ("v", half, tc)
                            plan.label = f"v{half}{tc}"
                        plan.op("tensor", lambda e, ph=ph, tc=tc, half=half, dc=dc: e.matmul(
                            ph[:], xn_sb[:, dc, 128 * tc:128 * (tc + 1)],
                            W2[:, dc, 512 * half:512 * (half + 1)],
                            start=(dc == 0), stop=(dc == NDC - 1)),
                            [rW2[dc], rXn[dc]], [pr])
                    plan.op("vector", lambda e, ph=ph, tc=tc, half=half: e.tensor_add(
                        vaug[:, tc, 8 * half:8 * (half + 1), 0:64],
                        ph[:].rearrange("p (h x) -> p h x", h=8),
                        bvb_sb[:, 512 * half:512 * (half + 1)].rearrange(
                            "p (h x) -> p h x", h=8)),
                        [pr, rBvb], [rVa[tc][half]])

        def g_qt2(s):
            xb, rXbc = (xb0, rXb[0]) if s % 2 == 0 else (xb1, rXb[1])
            qbanks = ((pa, rPa), (pb, rPb), (px1, rPx1), (px2, rPx2))
            for ocb in range(2):
                for dc in range(NDC):
                    yield ("qt", ocb, dc)
                    for j in range(4):
                        oc = 4 * ocb + j
                        ph, pr = qbanks[j]
                        plan.op("tensor", lambda e, ph=ph, dc=dc, oc=oc: e.matmul(
                            ph[:], W1[:, dc, 128 * oc:128 * (oc + 1)], xn_sb[:, dc, :],
                            start=(dc == 0), stop=(dc == NDC - 1)),
                            [rW1[dc], rXn[dc]], [pr])
                yield ("qtev", ocb)
                for j in range(4):
                    oc = 4 * ocb + j
                    ph, pr = qbanks[j]
                    plan.op("vector", lambda e, ph=ph, oc=oc: e.tensor_scalar(
                        kt_sb[:, oc, :], ph[:], bq_sb[:, oc:oc + 1], None, ALU.add),
                        [pr, rC128], [rKt[oc]])
            yield ("sT",)
            for dc in range(NDC):
                plan.op("tensor", lambda e, dc=dc: e.matmul(
                    pa[0:32, :], vr_sb[:, dc, :], xn_sb[:, dc, :],
                    start=(dc == 0), stop=(dc == NDC - 1)),
                    [rVr, rXn[dc]], [rPa])
            plan.op("vector", lambda e: e.tensor_copy(sT_sb[:], pa[0:32, :]),
                    [rPa], [rST])

        def g_sT():
            yield ("sT",)
            for dc in range(NDC):
                plan.op("tensor", lambda e, dc=dc: e.matmul(
                    pa[0:32, :], vr_sb[:, dc, :], xn_sb[:, dc, :],
                    start=(dc == 0), stop=(dc == NDC - 1)),
                    [rVr, rXn[dc]], [rPa])
            plan.op("vector", lambda e: e.tensor_copy(sT_sb[:], pa[0:32, :]),
                    [rPa], [rST])

        def _feat_ctx(h, tcg, kpx, rkpx):
            ctxp, rctxp = (px1, rPx1) if h % 2 == 0 else (px2, rPx2)
            for j in range(2):
                tc = 2 * tcg + j
                plan.op("tensor", lambda e, ctxp=ctxp, tc=tc, h=h, kpx=kpx, j=j: e.matmul(
                    ctxp[0:65, 0:FDIM], vaug[:, tc, h, :],
                    kpx[:, FDIM * j:FDIM * (j + 1)],
                    start=(tc == 0), stop=(tc == NTC - 1)),
                    [rVa[tc][h // 8], rkpx], [rctxp])
            if tcg == 1:
                plan.op("vector", lambda e, ctxp=ctxp, h=h: e.tensor_add(
                    ctxacc[0:65, FDIM * h:FDIM * (h + 1)],
                    ctxacc[0:65, FDIM * h:FDIM * (h + 1)], ctxp[0:65, 0:FDIM]),
                    [rctxp, rCtx], [rCtx])

        def g_feats():
            from collections import deque
            pend = deque()
            it = 0
            for h in range(NH):
                pr_i, hh = h // 2, h % 2
                for tcg in range(2):
                    yield ("feat", it)
                    plan.label = f"feat{it}"
                    kfp, rkfp = (pc, rPc) if it % 2 == 0 else (pd, rPd)
                    kpx, rkpx = ((kp0, rKp[0]), (kp1, rKp[1]), (kp2, rKp[2]), (kp3, rKp[3]))[it % 4]
                    for j in range(2):
                        tc = 2 * tcg + j
                        plan.op("tensor", lambda e, kfp=kfp, hh=hh, pr_i=pr_i, tc=tc, j=j: e.matmul(
                            kfp[:, FDIM * j:FDIM * (j + 1)],
                            kt_sb[64 * hh:64 * (hh + 1), pr_i, 128 * tc:128 * (tc + 1)],
                            wkf_sb[64 * hh:64 * (hh + 1), :],
                            start=True, stop=True),
                            [rKt[pr_i], rCB], [rkfp])
                    plan.op("scalar", lambda e, kpx=kpx, kfp=kfp: e.activation(
                        kpx[:], kfp[:], AF.Exp), [rkfp], [rkpx])
                    pend.append((h, tcg, kpx, rkpx))
                    if len(pend) > 2:
                        _feat_ctx(*pend.popleft())
                    it += 1
            while pend:
                yield ("featdrain", it)
                plan.label = f"fdrain{it}"
                it += 1
                _feat_ctx(*pend.popleft())

        def g_trans():
            yield ("wload",)
            for dc in range(NDC):
                dma("gpsimd", W1[:, dc, :], d_wq[128 * dc:128 * (dc + 1), :], [], [rW1[dc]])
                dma("gpsimd", W2[:, dc, :], d_wproj[128 * dc:128 * (dc + 1), :], [], [rW2[dc]])
                dma("gpsimd", W3[:, dc, :], d_wffnT[128 * dc:128 * (dc + 1), :], [], [rW3[dc]])
            for h in range(NH):
                yield ("tp", h)
                for fc in range(2):
                    ph, pr = (pa, rPa) if fc == 0 else (pb, rPb)
                    ctxF, rcF = (ctxF0, rCtxF[0]) if fc == 0 else (ctxF1, rCtxF[1])
                    plan.op("tensor", lambda e, ph=ph, h=h, fc=fc: e.transpose(
                        ph[:, 0:128], ctxacc[:, FDIM * h + 128 * fc:FDIM * h + 128 * (fc + 1)],
                        idn_sb), [rCtx, rC128], [pr])
                    plan.op("vector", lambda e, ph=ph, ctxF=ctxF, h=h: e.tensor_copy(
                        ctxF[:, 65 * h:65 * (h + 1)], ph[:, 0:65]), [pr], [rcF])

        def ffn_part_t0():
            for dc in range(NDC):
                plan.op("tensor", lambda e, dc=dc: e.matmul(
                    px1[:], W3[:, dc, 256:384], xn_sb[:, dc, :],
                    start=(dc == 0), stop=False),
                    [rW3, rXn[dc]], [rPx1])
            plan.op("tensor", lambda e: e.matmul(
                px1[:], ct_sb[:, 256:384], sT_sb[:],
                start=False, stop=False),
                [rC32, rST], [rPx1])

        def heads_front(h):
            pr_i, hh = h // 2, h % 2
            banks = ((pc, rPc), (pd, rPd)) if h % 2 == 0 else ((px1, rPx1), (px2, rPx2))
            for fc in range(2):
                qfp, rqfp = banks[fc]
                qpx = qps_all[2 * (h % 3) + fc]
                rqpx = rQp[2 * (h % 3) + fc]
                plan.op("tensor", lambda e, qfp=qfp, hh=hh, fc=fc, pr_i=pr_i: e.matmul(
                    qfp[:], wqf_sb[64 * hh:64 * (hh + 1), 128 * fc:128 * (fc + 1)],
                    kt_sb[64 * hh:64 * (hh + 1), pr_i, :],
                    start=True, stop=True),
                    [rKt[pr_i], rCB], [rqfp])
                plan.op("scalar", lambda e, qpx=qpx, qfp=qfp: e.activation(
                    qpx[:], qfp[:], AF.Exp), [rqfp], [rqpx])

        def heads_numevac(h):
            nump, rnump = (prb, rPrb) if h % 2 == 0 else (pmrb, rPmrb)
            for fc in range(2):
                qpx = qps_all[2 * (h % 3) + fc]
                rqpx = rQp[2 * (h % 3) + fc]
                ctxF, rcF = (ctxF0, rCtxF[0]) if fc == 0 else (ctxF1, rCtxF[1])
                plan.op("tensor", lambda e, nump=nump, ctxF=ctxF, h=h, qpx=qpx, fc=fc: e.matmul(
                    nump[0:65, :], ctxF[:, 65 * h:65 * (h + 1)], qpx[:],
                    start=(fc == 0), stop=(fc == 1)),
                    [rcF, rqpx], [rnump])
            plan.op("vector", lambda e, nump=nump, h=h: e.tensor_copy(
                n16[:, h, :], nump[0:65, :]), [rnump], [rN16[h]])
            if h % 4 == 3:
                q4 = h // 4
                p0 = 32 * q4
                dma("gpsimd", D16[p0:p0 + 4, :],
                    n16[64:65, 4 * q4:4 * (q4 + 1), :],
                    [rN16[4 * q4], rN16[4 * q4 + 1], rN16[4 * q4 + 2], rN16[4 * q4 + 3]],
                    [rD16[q4]])

        def heads_rows(q4):
            sl = slice(32 * q4, 32 * q4 + 4)
            plan.op("vector", lambda e, sl=sl: e.tensor_scalar_add(
                Di16[sl, :], D16[sl, :], 1e-6), [rD16[q4]], [rDi])
            plan.op("scalar", lambda e, sl=sl: e.activation(
                D16[sl, :], Di16[sl, :], AF.Ln), [rDi], [rD16[q4]])
            plan.op("scalar", lambda e, sl=sl: e.activation(
                Di16[sl, :], D16[sl, :], AF.Exp, scale=-1.0), [rD16[q4]], [rDi])
            plan.op("vector", lambda e, sl=sl: e.tensor_copy(HI16[sl, :], Di16[sl, :]),
                    [rDi], [rHI])
            plan.op("vector", lambda e, sl=sl: e.tensor_sub(
                LO16[sl, :], Di16[sl, :], HI16[sl, :]), [rDi, rHI], [rLO])
            a = 4 * q4
            dma("gpsimd", HILO[a:a + 4, :], HI16[sl, :], [rHI], [rHILO[q4][0]])
            dma("gpsimd", HILO[16 + a:16 + a + 4, :], LO16[sl, :], [rLO], [rHILO[q4][1]])

        def warm_mm(n, bank, rbank):
            for _ in range(n):
                plan.op("tensor", lambda e, bank=bank: e.matmul(
                    bank[0:1, :], oc_sb, W1[:, 0, 0:T], start=True, stop=True),
                    [rW1, rCB], [rbank])

        def heads_sel_mult(h):
            dbp, rdbp = (prb, rPrb) if h % 2 == 0 else (pmrb, rPmrb)
            plan.op("tensor", lambda e, dbp=dbp, h=h: e.matmul(
                dbp[0:64, :], e32_sb[:, 64 * h:64 * (h + 1)], HILO[:],
                start=True, stop=True), [rC32, rHILO[h // 4][0], rHILO[h // 4][1]], [rdbp])
            oc = h // 2
            if h % 2 == 0:
                plan.op("vector", lambda e, dbp=dbp, h=h, oc=oc: e.tensor_mul(
                    yT[0:64, oc, :], n16[0:64, h, :], dbp[0:64, :]),
                    [rN16[h], rdbp], [rYe[oc]])
            else:
                plan.op("vector", lambda e, dbp=dbp, h=h, oc=oc: e.tensor_mul(
                    yodd[:, oc, :], n16[0:64, h, :], dbp[0:64, :]),
                    [rN16[h], rdbp], [rYo[oc]])

        def g_heads():
            for h in range(NH):
                yield ("head", h)
                heads_front(h)
                if 4 <= h < 12:
                    oc = (h - 4) // 4
                    ph, pr = (pa, rPa) if oc == 0 else (pb, rPb)
                    j0 = ((h - 4) % 4) * 2
                    for dc in (j0, j0 + 1):
                        if dc < NDC:
                            plan.op("tensor", lambda e, ph=ph, dc=dc, oc=oc: e.matmul(
                                ph[:], W3[:, dc, 128 * oc:128 * (oc + 1)], xn_sb[:, dc, :],
                                start=(dc == 0), stop=False),
                                [rW3[dc], rXn[dc]], [pr])
                    if (h - 4) % 4 == 3:
                        plan.op("tensor", lambda e, ph=ph, oc=oc: e.matmul(
                            ph[:], ct_sb[:, 128 * oc:128 * (oc + 1)], sT_sb[:],
                            start=False, stop=False),
                            [rC32, rST], [pr])
                if h >= 2:
                    heads_numevac(h - 2)
                if h == NH - 1:
                    heads_numevac(h - 1)
                if h >= 5 and h % 4 == 1:
                    heads_rows((h - 5) // 4)
            yield ("t0",)
            heads_numevac(15)
            heads_rows(3)

        def g_out(s, st=None, s_next=None):
            xb, rXbc = (xb0, rXb[0]) if s % 2 == 0 else (xb1, rXb[1])
            obank = (pa, pb, px1, px2, pc, pd)
            orbank = (rPa, rPb, rPx1, rPx2, rPc, rPd)

            def ffn_part(oc, ph, pr):
                for dc in range(NDC):
                    plan.op("tensor", lambda e, ph=ph, dc=dc, oc=oc: e.matmul(
                        ph[:], W3[:, dc, 128 * oc:128 * (oc + 1)], xn_sb[:, dc, :],
                        start=(dc == 0), stop=False),
                        [rW3[dc], rXn[dc]], [pr])
                plan.op("tensor", lambda e, ph=ph, oc=oc: e.matmul(
                    ph[:], ct_sb[:, 128 * oc:128 * (oc + 1)], sT_sb[:],
                    start=False, stop=False),
                    [rC32, rST], [pr])

            def proj_part(oc, ph, pr):
                for hc in range(NDC):
                    plan.op("tensor", lambda e, ph=ph, hc=hc, oc=oc: e.matmul(
                        ph[:], W2[:, hc, 128 * oc:128 * (oc + 1)], yT[:, hc, :],
                        start=False, stop=(hc == NDC - 1)),
                        [rW2[hc], rYe[hc], rYoD[hc // 4]], [pr])

            def evac(oc, ph, pr):
                osb, rosb = (osb0, rOsb[0]) if oc % 2 == 0 else (osb1, rOsb[1])
                plan.op("vector", lambda e, ph=ph, osb=osb, xb=xb, oc=oc: e.scalar_tensor_tensor(
                    osb[:], ph[:], bsum_sb[:, oc:oc + 1], xb[:, oc, :],
                    ALU.add, ALU.add), [pr, rBsum, rXbc[oc]], [rosb])
                dma("sync", outT[128 * oc:128 * (oc + 1), T * s:T * (s + 1)], osb[:],
                    [rosb], [rOut] if (s == NSC - 1 and oc == 7) else [])

            for oc in range(2, 6):
                yield ("outA", oc)
                if oc >= 3 and st is not None and st.alive:
                    st.step()
                ffn_part(oc, obank[oc], orbank[oc])
                if oc == 3:
                    for h in range(NH):
                        heads_sel_mult(h)
                        if h == 7:
                            dma("gpsimd", yT[64:128, 0:4, :], yodd[:, 0:4, :],
                                [rYo[0], rYo[1], rYo[2], rYo[3]], [rYoD[0]])
                    dma("gpsimd", yT[64:128, 4:8, :], yodd[:, 4:8, :],
                        [rYo[4], rYo[5], rYo[6], rYo[7]], [rYoD[1]])
            for oc in range(6):
                yield ("outB", oc)
                if st is not None and st.alive:
                    st.step()
                proj_part(oc, obank[oc], orbank[oc])
                evac(oc, obank[oc], orbank[oc])
                if oc == 2:
                    yield ("outCf", 6)
                    if st is not None and st.alive:
                        st.step()
                    ffn_part(6, pa, rPa)
                if oc == 3:
                    yield ("outCf", 7)
                    if st is not None and st.alive:
                        st.step()
                    ffn_part(7, pb, rPb)
                    if s_next is not None:
                        for c in range(NDC):
                            dma("gpsimd", xn_sb[:, c, :],
                                d_xn[128 * c:128 * (c + 1), T * s_next:T * (s_next + 1)],
                                [rXnD[s_next][c]], [rXn[c]])
            for oc in (6, 7):
                yield ("outC", oc)
                if st is not None and st.alive:
                    st.step()
                ph, pr = (pa, rPa) if oc == 6 else (pb, rPb)
                proj_part(oc, ph, pr)
                evac(oc, ph, pr)

        class Stepper:
            def __init__(self, gen):
                self.gen = gen
                self.alive = True
                self.tag = None
                self._adv()

            def _adv(self):
                try:
                    self.tag = next(self.gen)
                except StopIteration:
                    self.alive = False
                    self.tag = None

            def step(self):
                self._adv()

        def chain(*gens):
            for g in gens:
                for t in g:
                    yield t

        def drain(gen):
            for _ in gen:
                pass

        def drive(lgen, fgen, lready, wl, wf):
            l = Stepper(lgen); f = Stepper(fgen)
            cl = cf = 0.0
            fprog = 0
            while l.alive or f.alive:
                want_l = (l.alive and (not f.alive or cl * wf <= cf * wl)
                          and lready(l.tag, fprog, f.alive))
                if want_l:
                    l.step(); cl += 1
                elif f.alive:
                    f.step(); fprog += 1; cf += 1
                elif l.alive:
                    l.step(); cl += 1

        def p1_ready(tag, fprog, falive):
            if not falive:
                return True
            if tag[0] == "kT":
                return fprog >= 4 * tag[1] + 4
            if tag[0] == "v":
                return fprog >= (18, 34)[tag[1]]
            if tag[0] == "bcast":
                return fprog >= 8
            return True

        def tr_ready(tag, fprog, falive):
            if not falive:
                return True
            if tag[0] == "tp":
                return fprog >= min(34, 2 * tag[1] + 4)
            return True

        drain(chain(g_stats(0), g_apply(0), g_proj(W1, rW1, bk_sb), g_v()))
        for s in range(1, NSC):
            drive(chain(g_stats(s), g_apply(s), g_proj(W1, rW1, bk_sb), g_v()),
                  g_feats(), p1_ready, 71, 34)
        drive(g_trans(), g_feats(), tr_ready, 17, 32)
        drain(g_stats(0, pass2=True))
        for c in range(NDC):
            dma("gpsimd", xn_sb[:, c, :], d_xn[128 * c:128 * (c + 1), 0:T],
                [rXnD[0][c]], [rXn[c]])
        drain(g_qt2(0))
        for s in range(NSC):
            drain(g_heads())
            st = Stepper(g_stats(s + 1, pass2=True)) if s < NSC - 1 else None
            drain(g_out(s, st, s + 1 if s < NSC - 1 else None))
            if s < NSC - 1:
                while st.alive:
                    st.step()
                drain(g_qt2(s + 1))

        plan.op("sync", None, [rOut], [])

        plan.finalize()

        sem_of = {"tensor": s_t, "vector": s_v, "scalar": s_s,
                  "sync": s_x, "gpsimd": s_g}

        def resolve(i):
            o = plan.ops[i]
            if o.dma:
                return (o.eng, 16 * o.dmarank)
            return (o.eng, o.semval)

        def emit(engine_name, eng):
            hw = {}
            for o in plan.ops:
                if o.eng != engine_name:
                    continue
                targets = {}
                for wi in o.waits:
                    sname, val = resolve(wi)
                    if val is None:
                        continue
                    if sname == engine_name and not plan.ops[wi].dma:
                        continue
                    if val > targets.get(sname, -1):
                        targets[sname] = val
                for sname, val in sorted(targets.items()):
                    if val > hw.get(sname, 0):
                        eng.wait_ge(sem_of[sname], val)
                        hw[sname] = val
                if o.fn is None:
                    continue
                inst = o.fn(eng)
                if o.dma:
                    inst.then_inc(sem_of[o.eng], 16)
                elif o.mark:
                    inst.then_inc(sem_of[o.eng], 1)

        @blk.sync
        def _(sync):
            emit("sync", sync)

        @blk.gpsimd
        def _(gp):
            emit("gpsimd", gp)

        @blk.tensor
        def _(tn):
            emit("tensor", tn)

        @blk.vector
        def _(vc):
            emit("vector", vc)

        @blk.scalar
        def _(sc):
            emit("scalar", sc)

    return nc


_NC_CACHE = None


def kernel(L0_attn_x, L0_attn_Wq, L0_attn_bq, L0_attn_Wk, L0_attn_bk,
           L0_attn_Wv, L0_attn_bv, L0_attn_Wproj, L0_attn_bproj,
           L0_attn_wq_feat, L0_attn_wk_feat, L0_ffn_W, L0_ffn_b,
           L0_ffn_V_r, L0_ffn_C):
    global _NC_CACHE
    x = np.asarray(L0_attn_x, dtype=np.float32)
    f32 = lambda a: np.ascontiguousarray(np.asarray(a, dtype=np.float32))

    wq = f32(L0_attn_Wq); wk = f32(L0_attn_Wk); wv = f32(L0_attn_Wv)
    wproj = f32(L0_attn_Wproj)
    wffnT = np.ascontiguousarray(f32(L0_ffn_W).T)
    vr = f32(L0_ffn_V_r)
    ct = np.ascontiguousarray(f32(L0_ffn_C).T)
    wqf2 = np.ascontiguousarray(np.vstack([f32(L0_attn_wq_feat)] * 2))
    wkf2 = np.ascontiguousarray(np.vstack([f32(L0_attn_wk_feat)] * 2))

    def b2(v):
        return np.ascontiguousarray(f32(v).reshape(8, 128).T)

    bq2 = b2(L0_attn_bq); bk2 = b2(L0_attn_bk)
    bpj2 = b2(L0_attn_bproj); bfn2 = b2(L0_ffn_b)
    bv1 = f32(L0_attn_bv).reshape(1, D)

    e32 = np.zeros((32, NH * DH), np.float32)
    for h in range(NH):
        e32[h, DH * h:DH * (h + 1)] = 1.0
        e32[16 + h, DH * h:DH * (h + 1)] = 1.0
    cst128 = np.zeros((128, 161), np.float32)
    cst128[:, 0:128] = np.eye(128, dtype=np.float32)
    cst128[:, 128:136] = bq2
    cst128[:, 136:144] = bk2
    cst128[:, 144:152] = bpj2
    cst128[:, 152:160] = bfn2
    cst128[:, 160] = 1.0
    cstb = np.zeros((128, 513), np.float32)
    cstb[:, 0:256] = wqf2
    cstb[:, 256:512] = wkf2
    cstb[:, 512] = 1.0
    cst32 = np.zeros((32, 2048), np.float32)
    cst32[:, 0:1024] = e32
    cst32[:, 1024:2048] = ct
    cst1 = np.zeros((1, 1152), np.float32)
    cst1[0, 0:1024] = bv1[0]
    cst1[0, 1024:1152] = 1.0

    if _NC_CACHE is None:
        _NC_CACHE = build_nc()
    nc = _NC_CACHE

    shared = {
        "wq": wq, "wk": wk, "wv": wv, "wproj": wproj, "wffnT": wffnT,
        "vr": vr, "cst128": cst128, "cstb": cstb, "cst32": cst32, "cst1": cst1,
    }
    in_maps = []
    for b in range(8):
        m = dict(shared)
        m["xT"] = np.ascontiguousarray(x[b].T)
        in_maps.append(m)

    res = run_bass_kernel_spmd(nc, in_maps, core_ids=list(range(8)))
    out = np.empty((8, NTOK, D), np.float32)
    for b in range(8):
        out[b] = res.results[b]["outT"].T
    return out


# version 34
# speedup vs baseline: 1.0512x; 1.0338x over previous
import numpy as np
import concourse.bass as bass
import concourse.mybir as mybir
from concourse.bass_utils import run_bass_kernel_spmd

F32 = mybir.dt.float32
BF16 = mybir.dt.bfloat16
AF = mybir.ActivationFunctionType
ALU = mybir.AluOpType

D = 1024
NTOK = 8192
NSC = 16
T = 512
NH = 16
DH = 64
FDIM = 256
NDC = 8
NTC = 4


class _Res:
    __slots__ = ("w", "readers")

    def __init__(self):
        self.w = None
        self.readers = []


class _Op:
    __slots__ = ("eng", "fn", "waits", "dma", "mark", "semval", "dmarank", "label")

    def __init__(self, eng, fn, waits, dma):
        self.eng = eng
        self.fn = fn
        self.waits = waits
        self.dma = dma
        self.mark = False
        self.semval = None
        self.dmarank = None


class _Plan:
    def __init__(self):
        self.ops = []
        self.label = ""

    def op(self, eng, fn, reads=(), writes=(), dma=False):
        i = len(self.ops)
        waits = set()
        for r in reads:
            if r.w is not None:
                wo = self.ops[r.w]
                if dma or wo.dma or wo.eng != eng:
                    waits.add(r.w)
                    wo.mark = True
            r.readers.append(i)
        for r in writes:
            deps = ([] if r.w is None else [r.w]) + r.readers
            for d in deps:
                if d == i:
                    continue
                do = self.ops[d]
                if dma or do.dma or do.eng != eng:
                    waits.add(d)
                    do.mark = True
            r.w = i
            r.readers = []
        o = _Op(eng, fn, sorted(waits), dma)
        o.label = self.label
        self.ops.append(o)
        return i

    def finalize(self):
        cnt = {"tensor": 0, "vector": 0, "scalar": 0}
        dcnt = {"sync": 0, "gpsimd": 0, "scalar_dma": 0}
        for o in self.ops:
            if o.dma:
                dcnt[o.eng if o.eng != "scalar" else "scalar_dma"] += 1
                o.dmarank = dcnt[o.eng if o.eng != "scalar" else "scalar_dma"]
                o.mark = True
            elif o.eng in cnt:
                if o.mark:
                    cnt[o.eng] += 1
                    o.semval = cnt[o.eng]


def build_nc():
    nc = bass.Bass("TRN2", target_bir_lowering=False)

    xT = nc.dram_tensor("xT", [D, NTOK], F32, kind="ExternalInput").ap()
    outT = nc.dram_tensor("outT", [D, NTOK], F32, kind="ExternalOutput").ap()
    d_wq = nc.dram_tensor("wq", [D, D], F32, kind="ExternalInput").ap()
    d_wk = nc.dram_tensor("wk", [D, D], F32, kind="ExternalInput").ap()
    d_wv = nc.dram_tensor("wv", [D, D], F32, kind="ExternalInput").ap()
    d_wproj = nc.dram_tensor("wproj", [D, D], F32, kind="ExternalInput").ap()
    d_wffnT = nc.dram_tensor("wffnT", [D, D], F32, kind="ExternalInput").ap()
    d_vr = nc.dram_tensor("vr", [D, 32], F32, kind="ExternalInput").ap()
    d_cst128 = nc.dram_tensor("cst128", [128, 161], F32, kind="ExternalInput").ap()
    d_cstb = nc.dram_tensor("cstb", [128, 513], F32, kind="ExternalInput").ap()
    d_cst32 = nc.dram_tensor("cst32", [32, 2048], F32, kind="ExternalInput").ap()
    d_cst1 = nc.dram_tensor("cst1", [1, 1152], F32, kind="ExternalInput").ap()
    d_rows = nc.dram_tensor("rows_scratch", [NSC, 4, T], BF16, kind="Internal").ap()
    d_xn = nc.dram_tensor("xn_scratch", [D, NTOK], BF16, kind="Internal").ap()

    plan = _Plan()
    R = _Res

    import contextlib
    _es = contextlib.ExitStack()
    with _es:
        W1 = _es.enter_context(nc.sbuf_tensor("W1", [128, NDC, D], BF16))
        W2 = _es.enter_context(nc.sbuf_tensor("W2", [128, NDC, D], BF16))
        W3 = _es.enter_context(nc.sbuf_tensor("W3", [128, NDC, D], BF16))
        vr_sb = _es.enter_context(nc.sbuf_tensor("vr_sb", [128, NDC, 32], BF16))
        cst128 = _es.enter_context(nc.sbuf_tensor("cst128_sb", [128, 161], F32))
        cstb = _es.enter_context(nc.sbuf_tensor("cstb_sb", [128, 513], BF16))
        cst32 = _es.enter_context(nc.sbuf_tensor("cst32_sb", [32, 2048], BF16))
        cst1 = _es.enter_context(nc.sbuf_tensor("cst1_sb", [1, 1152], BF16))
        bvb_sb = _es.enter_context(nc.sbuf_tensor("bvb_sb", [128, D], BF16))
        idn_sb = cst128[:, 0:128]
        bq_sb = cst128[:, 128:136]
        bk_sb = cst128[:, 136:144]
        bpj_sb = cst128[:, 144:152]
        bfn_sb = cst128[:, 152:160]
        wqf_sb = cstb[:, 0:FDIM]
        wkf_sb = cstb[:, FDIM:2 * FDIM]
        oc_sb = cstb[:, 512:513]
        e32_sb = cst32[:, 0:NH * DH]
        ct_sb = cst32[:, NH * DH:2 * NH * DH]
        bv_sb = cst1[:, 0:D]
        or_sb = cst1[:, D:D + 128]
        xb0 = _es.enter_context(nc.sbuf_tensor("xb0", [128, NDC, T], F32))
        xb1 = _es.enter_context(nc.sbuf_tensor("xb1", [128, NDC, T], F32))
        xbf0 = _es.enter_context(nc.sbuf_tensor("xbf0", [128, T], BF16))
        xbf1 = _es.enter_context(nc.sbuf_tensor("xbf1", [128, T], BF16))
        sqr0 = _es.enter_context(nc.sbuf_tensor("sqr0", [128, T], BF16))
        sqr1 = _es.enter_context(nc.sbuf_tensor("sqr1", [128, T], BF16))
        rows_f = _es.enter_context(nc.sbuf_tensor("rows_f", [1, 3, T], F32))
        rows_b = _es.enter_context(nc.sbuf_tensor("rows_b", [1, 4, T], BF16))
        tmpA = _es.enter_context(nc.sbuf_tensor("tmpA", [128, T], F32))
        bsum_sb = _es.enter_context(nc.sbuf_tensor("bsum_sb", [128, 8], F32))
        xn_sb = _es.enter_context(nc.sbuf_tensor("xn_sb", [128, NDC, T], BF16))
        kt_sb = _es.enter_context(nc.sbuf_tensor("kt_sb", [128, NDC, T], BF16))
        vaug = _es.enter_context(nc.sbuf_tensor("vaug", [128, NTC, NH, 65], BF16))
        kp0 = _es.enter_context(nc.sbuf_tensor("kp0", [128, T], BF16))
        kp1 = _es.enter_context(nc.sbuf_tensor("kp1", [128, T], BF16))
        kp2 = _es.enter_context(nc.sbuf_tensor("kp2", [128, T], BF16))
        kp3 = _es.enter_context(nc.sbuf_tensor("kp3", [128, T], BF16))
        qps_all = [_es.enter_context(nc.sbuf_tensor(f"qp{i}", [128, T], BF16))
                   for i in range(6)]
        ctxacc = _es.enter_context(nc.sbuf_tensor("ctxacc", [128, NH * FDIM], F32))
        ctxF0 = _es.enter_context(nc.sbuf_tensor("ctxF0", [128, NH * 65], BF16))
        ctxF1 = _es.enter_context(nc.sbuf_tensor("ctxF1", [128, NH * 65], BF16))
        n16 = _es.enter_context(nc.sbuf_tensor("n16", [65, NH, T], BF16))
        D16 = _es.enter_context(nc.sbuf_tensor("D16", [100, T], F32))
        Di16 = _es.enter_context(nc.sbuf_tensor("Di16", [100, T], F32))
        HI16 = _es.enter_context(nc.sbuf_tensor("HI16", [100, T], BF16))
        LO16 = _es.enter_context(nc.sbuf_tensor("LO16", [100, T], BF16))
        HILO = _es.enter_context(nc.sbuf_tensor("HILO", [32, T], BF16))
        yT = _es.enter_context(nc.sbuf_tensor("yT", [128, NDC, T], BF16))
        yodd = _es.enter_context(nc.sbuf_tensor("yodd", [64, NDC, T], BF16))
        sT_sb = _es.enter_context(nc.sbuf_tensor("sT_sb", [32, T], BF16))
        osb0 = _es.enter_context(nc.sbuf_tensor("osb0", [128, T], F32))
        osb1 = _es.enter_context(nc.sbuf_tensor("osb1", [128, T], F32))
        pa = _es.enter_context(nc.psum_tensor("pa", [128, T], F32))
        pb = _es.enter_context(nc.psum_tensor("pb", [128, T], F32))
        pc = _es.enter_context(nc.psum_tensor("pc", [128, T], F32))
        pd = _es.enter_context(nc.psum_tensor("pd", [128, T], F32))
        prb = _es.enter_context(nc.psum_tensor("prb", [128, T], F32))
        pmrb = _es.enter_context(nc.psum_tensor("pmrb", [128, T], F32))
        px1 = _es.enter_context(nc.psum_tensor("px1", [128, T], F32))
        px2 = _es.enter_context(nc.psum_tensor("px2", [128, T], F32))
        s_t = _es.enter_context(nc.semaphore("s_t"))
        s_v = _es.enter_context(nc.semaphore("s_v"))
        s_s = _es.enter_context(nc.semaphore("s_s"))
        s_x = _es.enter_context(nc.semaphore("s_x"))
        s_g = _es.enter_context(nc.semaphore("s_g"))
        blk = _es.enter_context(nc.Block())

        rW1 = [R() for _ in range(NDC)]; rW2 = [R() for _ in range(NDC)]
        rW3 = [R() for _ in range(NDC)]; rVr = R()
        rC128 = R(); rCB = R(); rC32 = R(); rC1 = R()
        rBvb = R()
        rXb = [[R() for _ in range(NDC)] for _ in range(2)]
        rXbf = [R(), R()]; rSqr = [R(), R()]
        rRowF = [R() for _ in range(4)]
        rRowB = [R() for _ in range(4)]
        rTmpA = R(); rBsum = R()
        rXn = [R() for _ in range(NDC)]
        rXnD = [[R() for _ in range(NDC)] for _ in range(NSC)]
        rKt = [R() for _ in range(NDC)]
        rVa = [[R(), R()] for _ in range(NTC)]
        rKp = [R(), R(), R(), R()]
        rQp = [R() for _ in range(6)]
        rCtx = R(); rCtxF = [R(), R()]
        rN16 = [R() for _ in range(NH)]
        rD16 = [R() for _ in range(4)]; rDi = R(); rHI = R(); rLO = R(); rHILO = [(R(), R()) for _ in range(4)]
        rYe = [R() for _ in range(NDC)]
        rYo = [R() for _ in range(NDC)]
        rYoD = [R(), R()]
        rST = R()
        rOsb = [R(), R()]
        rOut = R()
        rPa = R(); rPb = R(); rPc = R(); rPd = R()
        rPrb = R(); rPmrb = R(); rPx1 = R(); rPx2 = R()

        def dma(eng, dst, src, reads, writes):
            plan.op(eng, lambda e, d=dst, s=src: e.dma_start(d, s), reads, writes, dma=True)

        dma("sync", cst128[:], d_cst128, [], [rC128])
        for dc in range(NDC):
            dma("gpsimd", W1[:, dc, :], d_wk[128 * dc:128 * (dc + 1), :], [], [rW1[dc]])
            dma("gpsimd", W2[:, dc, :], d_wv[128 * dc:128 * (dc + 1), :], [], [rW2[dc]])
            dma("gpsimd", vr_sb[:, dc, :], d_vr[128 * dc:128 * (dc + 1), :], [], [rVr])
        dma("gpsimd", cstb[:], d_cstb, [], [rCB])
        dma("gpsimd", cst32[:], d_cst32, [], [rC32])
        dma("gpsimd", cst1[:], d_cst1, [], [rC1])

        plan.op("vector", lambda e: e.memset(ctxacc[:], 0.0), [], [rCtx])
        plan.op("vector", lambda e: e.tensor_add(bsum_sb[:], bpj_sb, bfn_sb),
                [rC128], [rBsum])
        plan.op("vector", lambda e: e.memset(HILO[:], 0.0), [], [r for pr_ in rHILO for r in pr_])
        plan.op("vector", lambda e: e.memset(vaug[:, :, :, 64:65], 1.0), [rC1],
                [r for pair in rVa for r in pair])
        for half, ph, pr in ((0, px1, rPx1), (1, px2, rPx2)):
            plan.op("tensor",
                    lambda e, ph=ph, half=half: e.matmul(
                        ph[:], or_sb, bv_sb[0:1, 512 * half:512 * (half + 1)],
                        start=True, stop=True),
                    [rC1], [pr])
            plan.op("vector",
                    lambda e, ph=ph, half=half: e.tensor_copy(
                        bvb_sb[:, 512 * half:512 * (half + 1)], ph[:]),
                    [pr], [rBvb])


        def g_stats(s, pass2=False):
            xb, rXbc = (xb0, rXb[0]) if s % 2 == 0 else (xb1, rXb[1])
            yield ("dma",)
            for c in range(NDC):
                dma("sync", xb[:, c, :], xT[128 * c:128 * (c + 1), T * s:T * (s + 1)],
                    [], [rXbc[c]])
            if pass2:
                yield ("spacer",)
                return
            for cg in range(NDC // 2):
                yield ("stats", cg)
                plan.label = f"stats{s}"
                for c in (2 * cg, 2 * cg + 1):
                    xr, sr = (xbf0, sqr0) if c % 2 == 0 else (xbf1, sqr1)
                    rxr, rsr = rXbf[c % 2], rSqr[c % 2]
                    plan.op("scalar", lambda e, xr=xr, xb=xb, c=c: e.copy(
                        xr[:], xb[:, c, :]), [rXbc[c]], [rxr])
                    plan.op("vector", lambda e, sr=sr, xb=xb, c=c: e.tensor_mul(
                        sr[:], xb[:, c, :], xb[:, c, :]), [rXbc[c]], [rsr])
                    plan.op("tensor", lambda e, xr=xr, c=c: e.matmul(
                        prb[0:1, :], oc_sb, xr[:], start=(c == 0), stop=(c == NDC - 1)),
                        [rxr, rCB], [rPrb])
                    plan.op("tensor", lambda e, sr=sr, c=c: e.matmul(
                        pmrb[0:1, :], oc_sb, sr[:], start=(c == 0), stop=(c == NDC - 1)),
                        [rsr, rCB], [rPmrb])
            yield ("rows",)
            plan.label = f"rows{s}"
            mean = rows_f[0:1, 0, :]; slot1 = rows_f[0:1, 1, :]
            slot2 = rows_f[0:1, 2, :]
            plan.op("vector", lambda e: e.tensor_scalar_mul(mean, prb[0:1, :], 1.0 / D),
                    [rPrb], [rRowF[0]])
            plan.op("vector", lambda e: e.tensor_scalar(
                slot1, pmrb[0:1, :], 1.0 / D, 1e-5, ALU.mult, ALU.add),
                [rPmrb], [rRowF[1]])
            plan.op("vector", lambda e: e.tensor_mul(slot2, mean, mean),
                    [rRowF[0]], [rRowF[2]])
            plan.op("vector", lambda e: e.tensor_sub(slot1, slot1, slot2),
                    [rRowF[1], rRowF[2]], [rRowF[1]])
            plan.op("scalar", lambda e: e.activation(slot2, slot1, AF.Ln),
                    [rRowF[1]], [rRowF[2]])
            plan.op("scalar", lambda e: e.activation(slot1, slot2, AF.Exp, scale=-0.5),
                    [rRowF[2]], [rRowF[1]])
            plan.op("vector", lambda e: e.tensor_mul(slot2, mean, slot1),
                    [rRowF[0], rRowF[1]], [rRowF[2]])
            rstd = slot1; mr = slot2
            rhi = rows_b[0:1, 0, :]; rlo = rows_b[0:1, 1, :]
            mhi = rows_b[0:1, 2, :]; mlo = rows_b[0:1, 3, :]
            plan.op("vector", lambda e: e.tensor_copy(rhi, rstd), [rRowF[1]], [rRowB[0]])
            plan.op("vector", lambda e: e.tensor_sub(rlo, rstd, rhi),
                    [rRowF[1], rRowB[0]], [rRowB[1]])
            plan.op("vector", lambda e: e.tensor_copy(mhi, mr), [rRowF[2]], [rRowB[2]])
            plan.op("vector", lambda e: e.tensor_sub(mlo, mr, mhi),
                    [rRowF[2], rRowB[2]], [rRowB[3]])
            yield ("spacer",)
            yield ("bcast",)
            plan.label = f"bcast{s}"
            plan.op("tensor", lambda e: e.matmul(prb[:], or_sb, rhi, start=True, stop=False),
                    [rRowB[0], rC1], [rPrb])
            plan.op("tensor", lambda e: e.matmul(prb[:], or_sb, rlo, start=False, stop=True),
                    [rRowB[1], rC1], [rPrb])
            plan.op("tensor", lambda e: e.matmul(pmrb[:], or_sb, mhi, start=True, stop=False),
                    [rRowB[2], rC1], [rPmrb])
            plan.op("tensor", lambda e: e.matmul(pmrb[:], or_sb, mlo, start=False, stop=True),
                    [rRowB[3], rC1], [rPmrb])

        def g_apply(s):
            xb, rXbc = (xb0, rXb[0]) if s % 2 == 0 else (xb1, rXb[1])
            yield ("apply", 0)
            plan.label = f"apply{s}"
            for c in range(NDC):
                plan.op("vector", lambda e, xb=xb, c=c: e.tensor_mul(
                    tmpA[:], xb[:, c, :], prb[:]), [rXbc[c], rPrb], [rTmpA])
                plan.op("vector", lambda e, c=c: e.tensor_sub(
                    xn_sb[:, c, :], tmpA[:], pmrb[:]),
                    [rTmpA, rPmrb], [rXn[c]])
                dma("sync", d_xn[128 * c:128 * (c + 1), T * s:T * (s + 1)],
                    xn_sb[:, c, :], [rXn[c]], [rXnD[s][c]])

        def g_proj(proj_w, rProjW, proj_bias):
            for oc in range(NDC):
                ph, pr = (pa, rPa) if oc % 2 == 0 else (pb, rPb)
                for dc in range(NDC):
                    if dc % 2 == 0:
                        yield ("kT", oc)
                        plan.label = f"kT{oc}"
                    plan.op("tensor", lambda e, ph=ph, dc=dc, oc=oc: e.matmul(
                        ph[:], proj_w[:, dc, 128 * oc:128 * (oc + 1)], xn_sb[:, dc, :],
                        start=(dc == 0), stop=(dc == NDC - 1)),
                        [rProjW[dc], rXn[dc]], [pr])
                plan.op("vector", lambda e, ph=ph, oc=oc: e.tensor_scalar(
                    kt_sb[:, oc, :], ph[:], proj_bias[:, oc:oc + 1], None, ALU.add),
                    [pr, rC128], [rKt[oc]])

        def g_v():
            for half in range(2):
                for tc in range(NTC):
                    ph, pr = (pa, rPa) if (tc + half) % 2 == 0 else (pb, rPb)
                    for dc in range(NDC):
                        if dc % 2 == 0:
                            yield ("v", half, tc)
                            plan.label = f"v{half}{tc}"
                        plan.op("tensor", lambda e, ph=ph, tc=tc, half=half, dc=dc: e.matmul(
                            ph[:], xn_sb[:, dc, 128 * tc:128 * (tc + 1)],
                            W2[:, dc, 512 * half:512 * (half + 1)],
                            start=(dc == 0), stop=(dc == NDC - 1)),
                            [rW2[dc], rXn[dc]], [pr])
                    plan.op("vector", lambda e, ph=ph, tc=tc, half=half: e.tensor_add(
                        vaug[:, tc, 8 * half:8 * (half + 1), 0:64],
                        ph[:].rearrange("p (h x) -> p h x", h=8),
                        bvb_sb[:, 512 * half:512 * (half + 1)].rearrange(
                            "p (h x) -> p h x", h=8)),
                        [pr, rBvb], [rVa[tc][half]])

        def g_qt2(s):
            xb, rXbc = (xb0, rXb[0]) if s % 2 == 0 else (xb1, rXb[1])
            qbanks = ((pa, rPa), (pb, rPb), (px1, rPx1), (px2, rPx2))
            for ocb in range(2):
                for dc in range(NDC):
                    yield ("qt", ocb, dc)
                    for j in range(4):
                        oc = 4 * ocb + j
                        ph, pr = qbanks[j]
                        plan.op("tensor", lambda e, ph=ph, dc=dc, oc=oc: e.matmul(
                            ph[:], W1[:, dc, 128 * oc:128 * (oc + 1)], xn_sb[:, dc, :],
                            start=(dc == 0), stop=(dc == NDC - 1)),
                            [rW1[dc], rXn[dc]], [pr])
                yield ("qtev", ocb)
                for j in range(4):
                    oc = 4 * ocb + j
                    ph, pr = qbanks[j]
                    plan.op("vector", lambda e, ph=ph, oc=oc: e.tensor_scalar(
                        kt_sb[:, oc, :], ph[:], bq_sb[:, oc:oc + 1], None, ALU.add),
                        [pr, rC128], [rKt[oc]])
            yield ("sT",)
            for dc in range(NDC):
                plan.op("tensor", lambda e, dc=dc: e.matmul(
                    pa[0:32, :], vr_sb[:, dc, :], xn_sb[:, dc, :],
                    start=(dc == 0), stop=(dc == NDC - 1)),
                    [rVr, rXn[dc]], [rPa])
            plan.op("vector", lambda e: e.tensor_copy(sT_sb[:], pa[0:32, :]),
                    [rPa], [rST])

        def g_sT():
            yield ("sT",)
            for dc in range(NDC):
                plan.op("tensor", lambda e, dc=dc: e.matmul(
                    pa[0:32, :], vr_sb[:, dc, :], xn_sb[:, dc, :],
                    start=(dc == 0), stop=(dc == NDC - 1)),
                    [rVr, rXn[dc]], [rPa])
            plan.op("vector", lambda e: e.tensor_copy(sT_sb[:], pa[0:32, :]),
                    [rPa], [rST])

        def _feat_ctx(h, tcg, kpx, rkpx):
            ctxp, rctxp = (px1, rPx1) if h % 2 == 0 else (px2, rPx2)
            for j in range(2):
                tc = 2 * tcg + j
                plan.op("tensor", lambda e, ctxp=ctxp, tc=tc, h=h, kpx=kpx, j=j: e.matmul(
                    ctxp[0:65, 0:FDIM], vaug[:, tc, h, :],
                    kpx[:, FDIM * j:FDIM * (j + 1)],
                    start=(tc == 0), stop=(tc == NTC - 1)),
                    [rVa[tc][h // 8], rkpx], [rctxp])
            if tcg == 1:
                plan.op("vector", lambda e, ctxp=ctxp, h=h: e.tensor_add(
                    ctxacc[0:65, FDIM * h:FDIM * (h + 1)],
                    ctxacc[0:65, FDIM * h:FDIM * (h + 1)], ctxp[0:65, 0:FDIM]),
                    [rctxp, rCtx], [rCtx])

        def g_feats():
            from collections import deque
            pend = deque()
            it = 0
            for h in range(NH):
                pr_i, hh = h // 2, h % 2
                for tcg in range(2):
                    yield ("feat", it)
                    plan.label = f"feat{it}"
                    kfp, rkfp = (pc, rPc) if it % 2 == 0 else (pd, rPd)
                    kpx, rkpx = ((kp0, rKp[0]), (kp1, rKp[1]), (kp2, rKp[2]), (kp3, rKp[3]))[it % 4]
                    for j in range(2):
                        tc = 2 * tcg + j
                        plan.op("tensor", lambda e, kfp=kfp, hh=hh, pr_i=pr_i, tc=tc, j=j: e.matmul(
                            kfp[:, FDIM * j:FDIM * (j + 1)],
                            kt_sb[64 * hh:64 * (hh + 1), pr_i, 128 * tc:128 * (tc + 1)],
                            wkf_sb[64 * hh:64 * (hh + 1), :],
                            start=True, stop=True),
                            [rKt[pr_i], rCB], [rkfp])
                    plan.op("scalar", lambda e, kpx=kpx, kfp=kfp: e.activation(
                        kpx[:], kfp[:], AF.Exp), [rkfp], [rkpx])
                    pend.append((h, tcg, kpx, rkpx))
                    if len(pend) > 2:
                        _feat_ctx(*pend.popleft())
                    it += 1
            while pend:
                yield ("featdrain", it)
                plan.label = f"fdrain{it}"
                it += 1
                _feat_ctx(*pend.popleft())

        def g_trans():
            yield ("wload",)
            for dc in range(NDC):
                dma("gpsimd", W1[:, dc, :], d_wq[128 * dc:128 * (dc + 1), :], [], [rW1[dc]])
                dma("gpsimd", W2[:, dc, :], d_wproj[128 * dc:128 * (dc + 1), :], [], [rW2[dc]])
                dma("gpsimd", W3[:, dc, :], d_wffnT[128 * dc:128 * (dc + 1), :], [], [rW3[dc]])
            for h in range(NH):
                yield ("tp", h)
                for fc in range(2):
                    ph, pr = (pa, rPa) if fc == 0 else (pb, rPb)
                    ctxF, rcF = (ctxF0, rCtxF[0]) if fc == 0 else (ctxF1, rCtxF[1])
                    plan.op("tensor", lambda e, ph=ph, h=h, fc=fc: e.transpose(
                        ph[:, 0:128], ctxacc[:, FDIM * h + 128 * fc:FDIM * h + 128 * (fc + 1)],
                        idn_sb), [rCtx, rC128], [pr])
                    plan.op("vector", lambda e, ph=ph, ctxF=ctxF, h=h: e.tensor_copy(
                        ctxF[:, 65 * h:65 * (h + 1)], ph[:, 0:65]), [pr], [rcF])

        def ffn_part_t0():
            for dc in range(NDC):
                plan.op("tensor", lambda e, dc=dc: e.matmul(
                    px1[:], W3[:, dc, 256:384], xn_sb[:, dc, :],
                    start=(dc == 0), stop=False),
                    [rW3, rXn[dc]], [rPx1])
            plan.op("tensor", lambda e: e.matmul(
                px1[:], ct_sb[:, 256:384], sT_sb[:],
                start=False, stop=False),
                [rC32, rST], [rPx1])

        def heads_front(h):
            pr_i, hh = h // 2, h % 2
            banks = ((pc, rPc), (pd, rPd)) if h % 2 == 0 else ((px1, rPx1), (px2, rPx2))
            for fc in range(2):
                qfp, rqfp = banks[fc]
                qpx = qps_all[2 * (h % 3) + fc]
                rqpx = rQp[2 * (h % 3) + fc]
                plan.op("tensor", lambda e, qfp=qfp, hh=hh, fc=fc, pr_i=pr_i: e.matmul(
                    qfp[:], wqf_sb[64 * hh:64 * (hh + 1), 128 * fc:128 * (fc + 1)],
                    kt_sb[64 * hh:64 * (hh + 1), pr_i, :],
                    start=True, stop=True),
                    [rKt[pr_i], rCB], [rqfp])
                plan.op("scalar", lambda e, qpx=qpx, qfp=qfp: e.activation(
                    qpx[:], qfp[:], AF.Exp), [rqfp], [rqpx])

        def heads_numevac(h):
            nump, rnump = (prb, rPrb) if h % 2 == 0 else (pmrb, rPmrb)
            for fc in range(2):
                qpx = qps_all[2 * (h % 3) + fc]
                rqpx = rQp[2 * (h % 3) + fc]
                ctxF, rcF = (ctxF0, rCtxF[0]) if fc == 0 else (ctxF1, rCtxF[1])
                plan.op("tensor", lambda e, nump=nump, ctxF=ctxF, h=h, qpx=qpx, fc=fc: e.matmul(
                    nump[0:65, :], ctxF[:, 65 * h:65 * (h + 1)], qpx[:],
                    start=(fc == 0), stop=(fc == 1)),
                    [rcF, rqpx], [rnump])
            plan.op("vector", lambda e, nump=nump, h=h: e.tensor_copy(
                n16[:, h, :], nump[0:65, :]), [rnump], [rN16[h]])
            if h % 4 == 3:
                q4 = h // 4
                p0 = 32 * q4
                dma("gpsimd", D16[p0:p0 + 4, :],
                    n16[64:65, 4 * q4:4 * (q4 + 1), :],
                    [rN16[4 * q4], rN16[4 * q4 + 1], rN16[4 * q4 + 2], rN16[4 * q4 + 3]],
                    [rD16[q4]])

        def heads_rows(q4):
            sl = slice(32 * q4, 32 * q4 + 4)
            plan.op("vector", lambda e, sl=sl: e.tensor_scalar_add(
                Di16[sl, :], D16[sl, :], 1e-6), [rD16[q4]], [rDi])
            plan.op("scalar", lambda e, sl=sl: e.activation(
                D16[sl, :], Di16[sl, :], AF.Ln), [rDi], [rD16[q4]])
            plan.op("scalar", lambda e, sl=sl: e.activation(
                Di16[sl, :], D16[sl, :], AF.Exp, scale=-1.0), [rD16[q4]], [rDi])
            plan.op("vector", lambda e, sl=sl: e.tensor_copy(HI16[sl, :], Di16[sl, :]),
                    [rDi], [rHI])
            plan.op("vector", lambda e, sl=sl: e.tensor_sub(
                LO16[sl, :], Di16[sl, :], HI16[sl, :]), [rDi, rHI], [rLO])
            a = 4 * q4
            dma("gpsimd", HILO[a:a + 4, :], HI16[sl, :], [rHI], [rHILO[q4][0]])
            dma("gpsimd", HILO[16 + a:16 + a + 4, :], LO16[sl, :], [rLO], [rHILO[q4][1]])

        def warm_mm(n, bank, rbank):
            for _ in range(n):
                plan.op("tensor", lambda e, bank=bank: e.matmul(
                    bank[0:1, :], oc_sb, W1[:, 0, 0:T], start=True, stop=True),
                    [rW1, rCB], [rbank])

        def heads_sel_mult(h):
            dbp, rdbp = (prb, rPrb) if h % 2 == 0 else (pmrb, rPmrb)
            plan.op("tensor", lambda e, dbp=dbp, h=h: e.matmul(
                dbp[0:64, :], e32_sb[:, 64 * h:64 * (h + 1)], HILO[:],
                start=True, stop=True), [rC32, rHILO[h // 4][0], rHILO[h // 4][1]], [rdbp])
            oc = h // 2
            if h % 2 == 0:
                plan.op("vector", lambda e, dbp=dbp, h=h, oc=oc: e.tensor_mul(
                    yT[0:64, oc, :], n16[0:64, h, :], dbp[0:64, :]),
                    [rN16[h], rdbp], [rYe[oc]])
            else:
                plan.op("vector", lambda e, dbp=dbp, h=h, oc=oc: e.tensor_mul(
                    yodd[:, oc, :], n16[0:64, h, :], dbp[0:64, :]),
                    [rN16[h], rdbp], [rYo[oc]])

        def g_heads():
            for h in range(NH):
                yield ("head", h)
                heads_front(h)
                if 4 <= h < 12:
                    oc = (h - 4) // 4
                    ph, pr = (pa, rPa) if oc == 0 else (pb, rPb)
                    j0 = ((h - 4) % 4) * 2
                    for dc in (j0, j0 + 1):
                        if dc < NDC:
                            plan.op("tensor", lambda e, ph=ph, dc=dc, oc=oc: e.matmul(
                                ph[:], W3[:, dc, 128 * oc:128 * (oc + 1)], xn_sb[:, dc, :],
                                start=(dc == 0), stop=False),
                                [rW3[dc], rXn[dc]], [pr])
                    if (h - 4) % 4 == 3:
                        plan.op("tensor", lambda e, ph=ph, oc=oc: e.matmul(
                            ph[:], ct_sb[:, 128 * oc:128 * (oc + 1)], sT_sb[:],
                            start=False, stop=False),
                            [rC32, rST], [pr])
                if h >= 2:
                    heads_numevac(h - 2)
                if h == NH - 1:
                    heads_numevac(h - 1)
            yield ("t0",)
            heads_rows(0)
            heads_rows(1)
            heads_rows(2)
            heads_numevac(15)
            heads_rows(3)

        def g_out(s, st=None, s_next=None):
            xb, rXbc = (xb0, rXb[0]) if s % 2 == 0 else (xb1, rXb[1])
            obank = (pa, pb, px1, px2, pc, pd)
            orbank = (rPa, rPb, rPx1, rPx2, rPc, rPd)

            def ffn_part(oc, ph, pr):
                for dc in range(NDC):
                    plan.op("tensor", lambda e, ph=ph, dc=dc, oc=oc: e.matmul(
                        ph[:], W3[:, dc, 128 * oc:128 * (oc + 1)], xn_sb[:, dc, :],
                        start=(dc == 0), stop=False),
                        [rW3[dc], rXn[dc]], [pr])
                plan.op("tensor", lambda e, ph=ph, oc=oc: e.matmul(
                    ph[:], ct_sb[:, 128 * oc:128 * (oc + 1)], sT_sb[:],
                    start=False, stop=False),
                    [rC32, rST], [pr])

            def proj_part(oc, ph, pr):
                for hc in range(NDC):
                    plan.op("tensor", lambda e, ph=ph, hc=hc, oc=oc: e.matmul(
                        ph[:], W2[:, hc, 128 * oc:128 * (oc + 1)], yT[:, hc, :],
                        start=False, stop=(hc == NDC - 1)),
                        [rW2[hc], rYe[hc], rYoD[hc // 4]], [pr])

            def evac(oc, ph, pr):
                osb, rosb = (osb0, rOsb[0]) if oc % 2 == 0 else (osb1, rOsb[1])
                plan.op("vector", lambda e, ph=ph, osb=osb, xb=xb, oc=oc: e.scalar_tensor_tensor(
                    osb[:], ph[:], bsum_sb[:, oc:oc + 1], xb[:, oc, :],
                    ALU.add, ALU.add), [pr, rBsum, rXbc[oc]], [rosb])
                dma("sync", outT[128 * oc:128 * (oc + 1), T * s:T * (s + 1)], osb[:],
                    [rosb], [rOut] if (s == NSC - 1 and oc == 7) else [])

            for oc in range(2, 6):
                yield ("outA", oc)
                if oc >= 3 and st is not None and st.alive:
                    st.step()
                ffn_part(oc, obank[oc], orbank[oc])
                if oc == 3:
                    for h in range(NH):
                        heads_sel_mult(h)
                        if h == 7:
                            dma("gpsimd", yT[64:128, 0:4, :], yodd[:, 0:4, :],
                                [rYo[0], rYo[1], rYo[2], rYo[3]], [rYoD[0]])
                    dma("gpsimd", yT[64:128, 4:8, :], yodd[:, 4:8, :],
                        [rYo[4], rYo[5], rYo[6], rYo[7]], [rYoD[1]])
            for oc in range(6):
                yield ("outB", oc)
                if st is not None and st.alive:
                    st.step()
                proj_part(oc, obank[oc], orbank[oc])
                evac(oc, obank[oc], orbank[oc])
                if oc == 2:
                    yield ("outCf", 6)
                    if st is not None and st.alive:
                        st.step()
                    ffn_part(6, pa, rPa)
                if oc == 3:
                    yield ("outCf", 7)
                    if st is not None and st.alive:
                        st.step()
                    ffn_part(7, pb, rPb)
                    if s_next is not None:
                        for c in range(NDC):
                            dma("gpsimd", xn_sb[:, c, :],
                                d_xn[128 * c:128 * (c + 1), T * s_next:T * (s_next + 1)],
                                [rXnD[s_next][c]], [rXn[c]])
            for oc in (6, 7):
                yield ("outC", oc)
                if st is not None and st.alive:
                    st.step()
                ph, pr = (pa, rPa) if oc == 6 else (pb, rPb)
                proj_part(oc, ph, pr)
                evac(oc, ph, pr)

        class Stepper:
            def __init__(self, gen):
                self.gen = gen
                self.alive = True
                self.tag = None
                self._adv()

            def _adv(self):
                try:
                    self.tag = next(self.gen)
                except StopIteration:
                    self.alive = False
                    self.tag = None

            def step(self):
                self._adv()

        def chain(*gens):
            for g in gens:
                for t in g:
                    yield t

        def drain(gen):
            for _ in gen:
                pass

        def drive(lgen, fgen, lready, wl, wf):
            l = Stepper(lgen); f = Stepper(fgen)
            cl = cf = 0.0
            fprog = 0
            while l.alive or f.alive:
                want_l = (l.alive and (not f.alive or cl * wf <= cf * wl)
                          and lready(l.tag, fprog, f.alive))
                if want_l:
                    l.step(); cl += 1
                elif f.alive:
                    f.step(); fprog += 1; cf += 1
                elif l.alive:
                    l.step(); cl += 1

        def p1_ready(tag, fprog, falive):
            if not falive:
                return True
            if tag[0] == "kT":
                return fprog >= 4 * tag[1] + 4
            if tag[0] == "v":
                return fprog >= (18, 34)[tag[1]]
            if tag[0] == "bcast":
                return fprog >= 8
            return True

        def tr_ready(tag, fprog, falive):
            if not falive:
                return True
            if tag[0] == "tp":
                return fprog >= min(34, 2 * tag[1] + 4)
            return True

        drain(chain(g_stats(0), g_apply(0), g_proj(W1, rW1, bk_sb), g_v()))
        for s in range(1, NSC):
            drive(chain(g_stats(s), g_apply(s), g_proj(W1, rW1, bk_sb), g_v()),
                  g_feats(), p1_ready, 71, 34)
        drive(g_trans(), g_feats(), tr_ready, 17, 32)
        drain(g_stats(0, pass2=True))
        for c in range(NDC):
            dma("gpsimd", xn_sb[:, c, :], d_xn[128 * c:128 * (c + 1), 0:T],
                [rXnD[0][c]], [rXn[c]])
        drain(g_qt2(0))
        for s in range(NSC):
            drain(g_heads())
            st = Stepper(g_stats(s + 1, pass2=True)) if s < NSC - 1 else None
            drain(g_out(s, st, s + 1 if s < NSC - 1 else None))
            if s < NSC - 1:
                while st.alive:
                    st.step()
                drain(g_qt2(s + 1))

        plan.op("sync", None, [rOut], [])

        plan.finalize()

        sem_of = {"tensor": s_t, "vector": s_v, "scalar": s_s,
                  "sync": s_x, "gpsimd": s_g}

        def resolve(i):
            o = plan.ops[i]
            if o.dma:
                return (o.eng, 16 * o.dmarank)
            return (o.eng, o.semval)

        def emit(engine_name, eng):
            hw = {}
            for o in plan.ops:
                if o.eng != engine_name:
                    continue
                targets = {}
                for wi in o.waits:
                    sname, val = resolve(wi)
                    if val is None:
                        continue
                    if sname == engine_name and not plan.ops[wi].dma:
                        continue
                    if val > targets.get(sname, -1):
                        targets[sname] = val
                for sname, val in sorted(targets.items()):
                    if val > hw.get(sname, 0):
                        eng.wait_ge(sem_of[sname], val)
                        hw[sname] = val
                if o.fn is None:
                    continue
                inst = o.fn(eng)
                if o.dma:
                    inst.then_inc(sem_of[o.eng], 16)
                elif o.mark:
                    inst.then_inc(sem_of[o.eng], 1)

        @blk.sync
        def _(sync):
            emit("sync", sync)

        @blk.gpsimd
        def _(gp):
            emit("gpsimd", gp)

        @blk.tensor
        def _(tn):
            emit("tensor", tn)

        @blk.vector
        def _(vc):
            emit("vector", vc)

        @blk.scalar
        def _(sc):
            emit("scalar", sc)

    return nc


_NC_CACHE = None


def kernel(L0_attn_x, L0_attn_Wq, L0_attn_bq, L0_attn_Wk, L0_attn_bk,
           L0_attn_Wv, L0_attn_bv, L0_attn_Wproj, L0_attn_bproj,
           L0_attn_wq_feat, L0_attn_wk_feat, L0_ffn_W, L0_ffn_b,
           L0_ffn_V_r, L0_ffn_C):
    global _NC_CACHE
    x = np.asarray(L0_attn_x, dtype=np.float32)
    f32 = lambda a: np.ascontiguousarray(np.asarray(a, dtype=np.float32))

    wq = f32(L0_attn_Wq); wk = f32(L0_attn_Wk); wv = f32(L0_attn_Wv)
    wproj = f32(L0_attn_Wproj)
    wffnT = np.ascontiguousarray(f32(L0_ffn_W).T)
    vr = f32(L0_ffn_V_r)
    ct = np.ascontiguousarray(f32(L0_ffn_C).T)
    wqf2 = np.ascontiguousarray(np.vstack([f32(L0_attn_wq_feat)] * 2))
    wkf2 = np.ascontiguousarray(np.vstack([f32(L0_attn_wk_feat)] * 2))

    def b2(v):
        return np.ascontiguousarray(f32(v).reshape(8, 128).T)

    bq2 = b2(L0_attn_bq); bk2 = b2(L0_attn_bk)
    bpj2 = b2(L0_attn_bproj); bfn2 = b2(L0_ffn_b)
    bv1 = f32(L0_attn_bv).reshape(1, D)

    e32 = np.zeros((32, NH * DH), np.float32)
    for h in range(NH):
        e32[h, DH * h:DH * (h + 1)] = 1.0
        e32[16 + h, DH * h:DH * (h + 1)] = 1.0
    cst128 = np.zeros((128, 161), np.float32)
    cst128[:, 0:128] = np.eye(128, dtype=np.float32)
    cst128[:, 128:136] = bq2
    cst128[:, 136:144] = bk2
    cst128[:, 144:152] = bpj2
    cst128[:, 152:160] = bfn2
    cst128[:, 160] = 1.0
    cstb = np.zeros((128, 513), np.float32)
    cstb[:, 0:256] = wqf2
    cstb[:, 256:512] = wkf2
    cstb[:, 512] = 1.0
    cst32 = np.zeros((32, 2048), np.float32)
    cst32[:, 0:1024] = e32
    cst32[:, 1024:2048] = ct
    cst1 = np.zeros((1, 1152), np.float32)
    cst1[0, 0:1024] = bv1[0]
    cst1[0, 1024:1152] = 1.0

    if _NC_CACHE is None:
        _NC_CACHE = build_nc()
    nc = _NC_CACHE

    shared = {
        "wq": wq, "wk": wk, "wv": wv, "wproj": wproj, "wffnT": wffnT,
        "vr": vr, "cst128": cst128, "cstb": cstb, "cst32": cst32, "cst1": cst1,
    }
    in_maps = []
    for b in range(8):
        m = dict(shared)
        m["xT"] = np.ascontiguousarray(x[b].T)
        in_maps.append(m)

    res = run_bass_kernel_spmd(nc, in_maps, core_ids=list(range(8)))
    out = np.empty((8, NTOK, D), np.float32)
    for b in range(8):
        out[b] = res.results[b]["outT"].T
    return out
